# version 26
# baseline (speedup 1.0000x reference)
import numpy as np
from contextlib import ExitStack

import concourse.bass as bass
import concourse.mybir as mybir
from concourse.bass_utils import run_bass_kernel_spmd

F32 = mybir.dt.float32
F16 = mybir.dt.float16
BF16 = mybir.dt.bfloat16
U32 = mybir.dt.uint32
Alu = mybir.AluOpType
Act = mybir.ActivationFunctionType
AX = mybir.AxisListType

T = 4096
D = 1024
NCC = 8
NTB = 32
NH = 8
DH = 64
NE = 8
P2TC = 256
NP2 = T // P2TC
ALL_PHASES = ("p0", "p1", "p2", "p4", "p6")

_DMA_KINDS = ("sync", "gpsimd_d", "scalar_d")
_ENGINE_OF = {"tensor": "tensor", "vector": "vector", "scalar": "scalar",
              "sync": "sync", "gpsimd_d": "gpsimd", "scalar_d": "scalar"}


def _region(ap):
    t = ap.tensor
    name = t.name
    apl = [list(x) for x in ap.ap]
    off = int(ap.offset)
    if t.__class__.__name__.startswith("DRam"):
        L = int(list(t.shape)[-1])
        r0, c0 = off // L, off % L
        rext = cext = 0
        ok = True
        for s, c in apl:
            s = abs(s)
            if s == 0 or c == 1:
                continue
            if s % L == 0:
                rext += (s // L) * (c - 1)
            elif s < L:
                cext += s * (c - 1)
            else:
                ok = False
        if ok and c0 + cext < L:
            return (name, r0, r0 + rext + 1, c0, c0 + cext + 1)
        return (name, 0, 1 << 60, 0, 1 << 60)
    import math
    pitch = 1
    for d in list(t.shape)[1:]:
        pitch *= int(d)
    p0 = off // pitch
    c0 = off % pitch
    pcnt = apl[0][1] if apl else 1
    pstep = apl[0][0] if apl else pitch
    if pstep % pitch == 0:
        pext = (pstep // pitch) * (pcnt - 1) + 1
    else:
        p0, pext = 0, 128
    cspan = sum(abs(s) * (c - 1) for s, c in apl[1:]) + 1
    return (name, p0, p0 + pext, c0, c0 + cspan)


def _overlap(a, b):
    return (a[0] == b[0] and a[1] < b[2] and b[1] < a[2]
            and a[3] < b[4] and b[3] < a[4])


class _Op:
    __slots__ = ("kind", "emit", "reads", "writes", "idx", "inc", "waits", "drain")

    def __init__(self, kind, emit, reads, writes, idx):
        self.kind = kind
        self.emit = emit
        self.reads = reads
        self.writes = writes
        self.idx = idx
        self.inc = False
        self.waits = {}
        self.drain = False


NPOOL = 12


class Prog:
    def __init__(self, nc, ctx):
        self.nc = nc
        self.sems = {k: ctx.enter_context(nc.semaphore(f"as_{k}"))
                     for k in _ENGINE_OF if k not in _DMA_KINDS}
        self.dsems = {k: [ctx.enter_context(nc.semaphore(f"ad_{k}{i}"))
                          for i in range(NPOOL)] for k in _DMA_KINDS}
        self.counts = {k: 0 for k in _ENGINE_OF}
        self.blocks = []
        self.cur = None

    def block(self):
        self.cur = []
        self.blocks.append(self.cur)

    def add(self, kind, emit, reads=(), writes=()):
        op = _Op(kind, emit,
                 [_region(a) for a in reads],
                 [_region(a) for a in writes],
                 self.counts[kind])
        self.counts[kind] += 1
        self.cur.append(op)
        return op

    def _analyze(self):
        for ops in self.blocks:
            writers = {}
            readers = {}
            last_drain = {}
            pos_of = {id(o): i for i, o in enumerate(ops)}
            for i, op in enumerate(ops):
                deps = []
                for r in op.reads:
                    for reg, w in writers.get(r[0], ()):
                        if _overlap(r, reg):
                            deps.append(w)
                for wr in op.writes:
                    for reg, w in writers.get(wr[0], ()):
                        if _overlap(wr, reg):
                            deps.append(w)
                    for reg, rd in readers.get(wr[0], ()):
                        if _overlap(wr, reg):
                            deps.append(rd)
                need_drain = False
                for d in deps:
                    if d is op:
                        continue
                    if d.kind == op.kind and d.kind not in _DMA_KINDS:
                        if op.kind in ("vector", "scalar"):
                            dp = pos_of[id(d)]
                            if dp >= last_drain.get(op.kind, 0):
                                need_drain = True
                    else:
                        key = (d.kind, d.idx % NPOOL) if d.kind in _DMA_KINDS \
                            else (d.kind, 0)
                        prev = op.waits.get(key, -1)
                        if d.idx > prev:
                            op.waits[key] = d.idx
                        d.inc = True
                if need_drain:
                    op.drain = True
                    last_drain[op.kind] = i
                for wr in op.writes:
                    writers.setdefault(wr[0], []).append((wr, op))
                for r in op.reads:
                    readers.setdefault(r[0], []).append((r, op))
        for ops in self.blocks:
            for op in ops:
                if op.kind in _DMA_KINDS:
                    op.inc = True
        self.rank = {k: {} for k in _ENGINE_OF if k not in _DMA_KINDS}
        inc_ops = {k: [] for k in self.rank}
        for ops in self.blocks:
            for op in ops:
                if op.inc and op.kind in self.rank:
                    inc_ops[op.kind].append(op.idx)
        for k, lst in inc_ops.items():
            lst.sort()
            for r, idx in enumerate(lst):
                self.rank[k][idx] = r + 1

    def _check_acyclic(self):
        for ops in self.blocks:
            n = len(ops)
            pos_of = {}
            by_kind = {}
            for i, op in enumerate(ops):
                pos_of[id(op)] = i
                by_kind.setdefault(op.kind, []).append(i)
            adj = [[] for _ in range(n)]
            indeg = [0] * n
            for lst in by_kind.values():
                for a, b in zip(lst, lst[1:]):
                    adj[a].append(b)
                    indeg[b] += 1
            idx2pos = {}
            for i, op in enumerate(ops):
                idx2pos[(op.kind, op.idx)] = i
            for i, op in enumerate(ops):
                for (k, _sl), tgt in op.waits.items():
                    j = idx2pos.get((k, tgt))
                    if j is not None:
                        adj[j].append(i)
                        indeg[i] += 1
            from collections import deque
            q = deque(i for i in range(n) if indeg[i] == 0)
            seen = 0
            while q:
                u = q.popleft()
                seen += 1
                for v in adj[u]:
                    indeg[v] -= 1
                    if indeg[v] == 0:
                        q.append(v)
            if seen != n:
                stuck = [(i, ops[i].kind, ops[i].idx, ops[i].waits)
                         for i in range(n) if indeg[i] > 0][:8]
                raise RuntimeError(f"cycle in schedule: {stuck}")

    def emit(self):
        self._analyze()
        self._check_acyclic()
        dma_done = {k: 0 for k in _DMA_KINDS}
        for ops in self.blocks:
            kinds = []
            for op in ops:
                if op.kind not in kinds:
                    kinds.append(op.kind)
            fence = dict(dma_done)
            eng_kinds = {}
            for k in kinds:
                eng_kinds.setdefault(_ENGINE_OF[k], []).append(k)
            with self.nc.Block() as blk:
                for engname, ks in eng_kinds.items():
                    kops = [o for o in ops if o.kind in ks]
                    getattr(blk, engname)(self._body(ks[0], kops, fence))
            for op in ops:
                if op.kind in _DMA_KINDS:
                    dma_done[op.kind] += 1
        with self.nc.Block() as blk:
            for k in _DMA_KINDS:
                total = self.counts[k]
                if total:
                    def mk(kk, tt):
                        def body(eng):
                            for sl in range(NPOOL):
                                n = tt // NPOOL + (1 if sl < tt % NPOOL else 0)
                                if n:
                                    eng.wait_ge(self.dsems[kk][sl], 16 * n)
                        return body
                    getattr(blk, _ENGINE_OF[k])(mk(k, total))

    def _body(self, kind, kops, fence):
        def body(eng):
            seen = {}
            def w(sem_h, val, key):
                if seen.get(key, -1) >= val:
                    return
                seen[key] = val
                eng.wait_ge(sem_h, val)
            for fk, cnt in fence.items():
                for sl in range(NPOOL):
                    n = cnt // NPOOL + (1 if sl < cnt % NPOOL else 0)
                    if n:
                        w(self.dsems[fk][sl], 16 * n, (fk, sl))
            for op in kops:
                for (wk, _sl), tgt in sorted(op.waits.items()):
                    if wk in _DMA_KINDS:
                        w(self.dsems[wk][tgt % NPOOL],
                          16 * (tgt // NPOOL + 1), (wk, tgt % NPOOL))
                    else:
                        r = self.rank[wk].get(tgt)
                        assert r is not None, (wk, tgt)
                        w(self.sems[wk], r, (wk, 0))
                if op.kind in _DMA_KINDS and op.idx >= NPOOL:
                    w(self.dsems[op.kind][op.idx % NPOOL],
                      16 * (op.idx // NPOOL), (op.kind, op.idx % NPOOL))
                if op.drain:
                    eng.drain()
                ins = op.emit(eng)
                if op.inc:
                    if op.kind in _DMA_KINDS:
                        ins.then_inc(self.dsems[op.kind][op.idx % NPOOL], 16)
                    else:
                        ins.then_inc(self.sems[op.kind], 1)
        return body


def _build(phases=ALL_PHASES, dbg=False):
    nc = bass.Bass("TRN2", target_bir_lowering=False)

    def din(name, shape, dt=F32):
        return nc.dram_tensor(name, list(shape), dt, kind="ExternalInput")

    x_d = din("x", (T, D))
    winT_d = din("winT", (D, 2 * D))
    wgT_d = din("wgT", (D, 2 * D))
    woT_d = din("woT", (D, D))
    wcv_d = din("wcv", (128, NCC, 4))
    bcv_d = din("bcv", (128, NCC))
    bga_d = din("bga", (128, 16))
    fb_d = din("fb", (128, NCC))
    wq_d = din("wq", (D, NH * DH))
    wk_d = din("wk", (D, NH * DH))
    wgate_d = din("wgate", (D, NH * NE))
    wv_d = din("wv", (D, NH * NE * DH))
    woe_d = din("woe", (NE, 4, 128, D))
    ident_d = din("c_ident", (128, 128))
    ones_d = din("c_ones", (128, 32))
    tri_d = din("c_tri", (4, 128, 512))

    y_d = nc.dram_tensor("y", [T, D], F32, kind="ExternalOutput")

    def scr(name, shape, dt, prod, cons):
        if prod in phases:
            kind = "ExternalOutput" if dbg else "Internal"
        elif any(c in phases for c in cons):
            kind = "ExternalInput"
        else:
            kind = "Internal"
        return nc.dram_tensor(name, list(shape), dt, kind=kind)

    sxln1_d = scr("s_xln1", (T, D), F16, "p0", ("p1",))
    sgelu_d = scr("s_gelu", (NCC, 128, T), F16, "p1", ("p2",))
    sxa_d = scr("s_xa", (NCC, 128, T), F16, "p1", ("p2",))
    sx1_d = scr("s_x1", (T, D), F32, "p2", ("p6",))
    sxln2_d = scr("s_xln2", (T, D), F16, "p2", ("p4",))
    sq_d = scr("s_q", (4, 128, T), F16, "p4", ("p6",))
    sgmf_d = scr("s_gmf", (NH * NE, T), F32, "p4", ("p6",))
    sms_d = scr("s_ms", (1, NH), F32, "p4", ("p6",))
    srs_d = scr("s_rs", (NH, T), F32, "p6", ("p6",))

    with ExitStack() as ctx:
        def sb(name, shape, dt=F32):
            return ctx.enter_context(nc.sbuf_tensor(name, list(shape), dt))

        def pst(name, shape, dt=F32):
            return ctx.enter_context(nc.psum_tensor(name, list(shape), dt))

        p = Prog(nc, ctx)

        cst = sb("cst", (128, 4))
        nc.const_aps.register(F32, 1e-5, cst[:, 0:1])
        nc.const_aps.register(F32, 1.000001, cst[:, 1:2])
        p.block()
        p.add("vector", lambda v: v.memset(cst[:, 0:1], 1e-5), writes=[cst[:, 0:1]])
        p.add("vector", lambda v: v.memset(cst[:, 1:2], 1.000001), writes=[cst[:, 1:2]])


        def dma(kind, dst, src, transpose=False):
            if transpose:
                fn = lambda e, d=dst, s=src: e.dma_start_transpose(d, s)
            else:
                fn = lambda e, d=dst, s=src: e.dma_start(d, s)
            return p.add(kind, fn, reads=[src], writes=[dst])

        def mmgrp(ps_, pairs, tile=None):
            def fn(t, ps_=ps_, pairs=pairs, tile=tile):
                n = len(pairs)
                for i, (l, r) in enumerate(pairs):
                    ins = t.matmul(ps_, l, r, start=(i == 0), stop=(i == n - 1),
                                   tile_position=tile)
                return ins
            rd = [a for pr in pairs for a in pr]
            return p.add("tensor", fn, reads=rd, writes=[ps_])

        if "p0" in phases or "p1" in phases:
            with ExitStack() as pc:
                psb = lambda n, s, d=F32: pc.enter_context(nc.sbuf_tensor(n, list(s), d))
                xt = [psb(f"p0x{i}", (128, D)) for i in range(8)]
                lnt = [psb(f"p0l{i}", (128, D), F16) for i in range(4)]
                bn4 = psb("p0b4", (128, 48))
                mv4 = psb("p0m4", (128, 8))
                sd4 = psb("p0d4", (128, 8))
                ri4 = psb("p0r4", (128, 8))
                win = psb("p1w", (128, NCC * 2048), F16)
                xlnf = psb("p1xf", (128, NCC * T), F16)
                gst = [psb(f"p1g{i}", (128, 8 * 512), F16) for i in range(2)]
                ast = [psb(f"p1a{i}", (128, 8 * 512), F16) for i in range(2)]
                pp = [pc.enter_context(nc.psum_tensor(f"p1p{i}", [128, 512], F32))
                      for i in range(2)]
                p.block()

                def p0_load(i):
                    dma("gpsimd_d", xt[i % 8][:, :], x_d.ap()[i * 128:(i + 1) * 128, :])

                def p0_qt(qt):
                    for k in range(4):
                        i = qt * 4 + k
                        b = i % 8
                        p.add("vector",
                              (lambda v, b=b, k=k: v.bn_stats(
                                  bn4[:, 12 * k:12 * k + 6], xt[b][:, 0:512])),
                              reads=[xt[b][:, 0:512]], writes=[bn4[:, 12 * k:12 * k + 6]])
                        p.add("vector",
                              (lambda v, b=b, k=k: v.bn_stats(
                                  bn4[:, 12 * k + 6:12 * k + 12], xt[b][:, 512:1024])),
                              reads=[xt[b][:, 512:1024]],
                              writes=[bn4[:, 12 * k + 6:12 * k + 12]])
                        p.add("vector",
                              (lambda v, k=k: v.bn_aggr(
                                  mv4[:, 2 * k:2 * k + 2], bn4[:, 12 * k:12 * k + 12])),
                              reads=[bn4[:, 12 * k:12 * k + 12]],
                              writes=[mv4[:, 2 * k:2 * k + 2]])
                    p.add("scalar",
                          lambda s: s.activation(
                              sd4[:, 0:4], bass.AP(mv4, 1, [[8, 128], [2, 4]]),
                              Act.Sqrt, bias=1e-5),
                          reads=[mv4[:, :], cst[:, 0:1]], writes=[sd4[:, 0:4]])
                    p.add("vector", lambda v: v.reciprocal(ri4[:, 0:4], sd4[:, 0:4]),
                          reads=[sd4[:, 0:4]], writes=[ri4[:, 0:4]])
                    for k in range(4):
                        i = qt * 4 + k
                        b = i % 8
                        p.add("vector",
                              (lambda v, b=b, k=k: v.tensor_scalar(
                                  lnt[k][:, :], xt[b][:, :], mv4[:, 2 * k:2 * k + 1],
                                  ri4[:, k:k + 1],
                                  Alu.subtract, Alu.mult)),
                              reads=[xt[b][:, :], mv4[:, 2 * k:2 * k + 1],
                                     ri4[:, k:k + 1]],
                              writes=[lnt[k][:, :]])
                        dma("gpsimd_d", sxln1_d.ap()[i * 128:(i + 1) * 128, :],
                            lnt[k][:, :])

                def p1_trans(c2):
                    r0 = c2 * 1024
                    for cc in range(NCC):
                        dma("sync", xlnf[:, cc * T + r0:cc * T + r0 + 1024],
                            sxln1_d.ap()[r0:r0 + 1024,
                                         cc * 128:(cc + 1) * 128], transpose=True)

                def p1_proj(tc):
                    b = tc % 2
                    for fc in range(16):
                        idx = tc * 16 + fc
                        ps_ = pp[idx % 2][:, :]
                        mmgrp(ps_, [
                            (win[:, cc * 2048 + fc * 128:cc * 2048 + (fc + 1) * 128],
                             xlnf[:, cc * T + tc * 512:cc * T + (tc + 1) * 512])
                            for cc in range(NCC)])
                        if fc < 8:
                            dst = gst[b][:, fc * 512:(fc + 1) * 512]
                            p.add("scalar",
                                  (lambda s, d=dst, ps2=ps_: s.activation(d, ps2, Act.Gelu)),
                                  reads=[ps_], writes=[dst])
                        else:
                            dst = ast[b][:, (fc - 8) * 512:(fc - 7) * 512]
                            p.add("scalar",
                                  (lambda s, d=dst, ps2=ps_: s.copy(d, ps2)),
                                  reads=[ps_], writes=[dst])
                    for fc8 in range(8):
                        dma("gpsimd_d", sgelu_d.ap()[fc8, :, tc * 512:(tc + 1) * 512],
                            gst[b][:, fc8 * 512:(fc8 + 1) * 512])
                    for fc8 in range(8):
                        dma("gpsimd_d", sxa_d.ap()[fc8, :, tc * 512:(tc + 1) * 512],
                            ast[b][:, fc8 * 512:(fc8 + 1) * 512])

                for cc in range(NCC):
                    dma("gpsimd_d", win[:, cc * 2048:(cc + 1) * 2048],
                        winT_d.ap()[cc * 128:(cc + 1) * 128, :])
                for k in range(8):
                    p0_load(k)
                for qt in range(10):
                    if qt < 8:
                        p0_qt(qt)
                        if qt + 2 < 8:
                            for k in range(4):
                                p0_load((qt + 2) * 4 + k)
                    if qt % 2 == 1 and (qt - 1) // 2 < 4:
                        p1_trans((qt - 1) // 2)
                    if 2 <= qt <= 9:
                        p1_proj(qt - 2)

        if "p2" in phases:
            with ExitStack() as pc:
                psb = lambda n, s, d=F32: pc.enter_context(nc.sbuf_tensor(n, list(s), d))
                wg = psb("p2wg", (128, NCC * 2048), F16)
                wo = psb("p2wo", (128, NCC * 1024), F16)
                wcv = psb("p2wc", (128, NCC * 4))
                bcv = psb("p2bc", (128, NCC))
                bga = psb("p2bg", (128, 16))
                fbt = psb("p2fb", (128, NCC))
                e1 = psb("p2e1", (128, NCC))
                spt = psb("p2sp", (128, NCC))
                aco = psb("p2ac", (128, NCC))
                W = NCC * P2TC
                xa = [psb(f"p2xa{i}", (128, NCC * 260), F16) for i in range(2)]
                gel = [psb(f"p2ge{i}", (128, W), F16) for i in range(2)]
                xcf = [psb(f"p2xc{i}", (128, W), F16) for i in range(2)]
                idn16 = psb("p2id", (128, 128), F16)
                dgt = psb("p2dg", (128, 32 * 128), F16)
                sf = [psb("p2sf0", (128, W))]
                si = [psb(f"p2si{i}", (128, W), F16) for i in range(2)]
                al = [psb(f"p2al{i}", (128, W)) for i in range(2)]
                bt = [psb(f"p2bt{i}", (128, W), F16) for i in range(2)]
                ub = psb("p2ub", (128, W), F16)
                hbuf = [psb(f"p2h{i}", (128, W)) for i in range(2)]
                yt = [psb(f"p2y{i}", (128, W), F16) for i in range(2)]
                xt2 = [psb(f"p2x{i}", (128, 2048)) for i in range(3)]
                x1t = [psb(f"p2o{i}", (128, 2048)) for i in range(2)]
                ln2t = [psb(f"p2n{i}", (128, 2048), F16) for i in range(2)]
                bn2 = psb("p2b2", (128, 24))
                mv2 = psb("p2m2", (128, 4))
                sd2 = psb("p2d2", (128, 2))
                ri2 = psb("p2r2", (128, 2))
                pg = [pc.enter_context(nc.psum_tensor(f"p2pg{i}", [128, 512], F32))
                      for i in range(2)]
                pcv = [pc.enter_context(nc.psum_tensor(f"p2pc{i}", [128, 512], F32))
                       for i in range(2)]
                po = [pc.enter_context(nc.psum_tensor(f"p2po{i}", [128, 512], F32))
                      for i in range(4)]
                p.block()
                for cc in range(NCC):
                    dma("gpsimd_d", wg[:, cc * 2048:(cc + 1) * 2048],
                        wgT_d.ap()[cc * 128:(cc + 1) * 128, :])
                for cc in range(NCC):
                    dma("gpsimd_d", wo[:, cc * 1024:(cc + 1) * 1024],
                        woT_d.ap()[cc * 128:(cc + 1) * 128, :])
                dma("gpsimd_d", wcv[:, :].rearrange("p (c k) -> p c k", c=NCC),
                    wcv_d.ap())
                dma("gpsimd_d", bcv[:, :], bcv_d.ap())
                dma("gpsimd_d", bga[:, :], bga_d.ap())
                dma("gpsimd_d", fbt[:, :], fb_d.ap())
                dma("gpsimd_d", idn16[:, :], ident_d.ap())
                p.add("scalar", lambda s: s.activation(e1[:, :], fbt[:, :], Act.Exp),
                      reads=[fbt[:, :]], writes=[e1[:, :]])
                p.add("scalar", lambda s: s.activation(spt[:, :], e1[:, :], Act.Ln, bias=1.0),
                      reads=[e1[:, :]], writes=[spt[:, :]])
                p.add("vector", lambda v: v.tensor_scalar(aco[:, :], spt[:, :], -8.0,
                                                          None, Alu.mult),
                      reads=[spt[:, :]], writes=[aco[:, :]])
                for cc in range(NCC):
                    p.add("vector",
                          (lambda v, cc=cc: v.memset(xa[0][:, cc * 260:cc * 260 + 3], 0.0)),
                          writes=[xa[0][:, cc * 260:cc * 260 + 3]])
                for cc in range(NCC):
                    for tap in range(4):
                        j = cc * 4 + tap
                        dst = dgt[:, j * 128:(j + 1) * 128]
                        p.add("vector",
                              (lambda v, d=dst, j=j: v.tensor_scalar(
                                  d, idn16[:, :], wcv[:, j:j + 1], None, Alu.mult)),
                              reads=[idn16[:, :], wcv[:, j:j + 1]], writes=[dst])

                def p2_loads_xa(tc):
                    b = tc % 2
                    for cc in range(NCC):
                        if tc == 0:
                            dma("sync", xa[b][:, cc * 260 + 3:cc * 260 + 259],
                                sxa_d.ap()[cc, :, 0:256])
                        else:
                            dma("sync", xa[b][:, cc * 260:cc * 260 + 259],
                                sxa_d.ap()[cc, :, tc * 256 - 3:tc * 256 + 256])
                    for tbi in range(2):
                        r0 = tc * 256 + tbi * 128
                        dma("sync", xt2[tc % 3][:, tbi * 1024:(tbi + 1) * 1024],
                            x_d.ap()[r0:r0 + 128, :])

                def p2_loads_gel(tc):
                    b = tc % 2
                    for cc in range(NCC):
                        dma("sync", gel[b][:, cc * 256:(cc + 1) * 256],
                            sgelu_d.ap()[cc, :, tc * 256:(tc + 1) * 256])

                def p2_conv(tc):
                    b = tc % 2
                    for cc in range(NCC):
                        xo = cc * 260
                        sl = slice(cc * 256, (cc + 1) * 256)
                        ps_ = pcv[cc % 2][:, 0:256]
                        mmgrp(ps_, [
                            (dgt[:, (cc * 4 + tap) * 128:(cc * 4 + tap + 1) * 128],
                             xa[b][:, xo + tap:xo + tap + 256])
                            for tap in range(4)])
                        p.add("scalar",
                              (lambda s, d=xcf[b][:, sl], ps2=ps_, cc=cc: s.activation(
                                  d, ps2, Act.Identity, bias=bcv[:, cc:cc + 1])),
                              reads=[ps_, bcv[:, cc:cc + 1]], writes=[xcf[b][:, sl]])

                def p2_gates(tc):
                    b = tc % 2
                    for fc in range(16):
                        idx = tc * 16 + fc
                        ps_ = pg[idx % 2][:, 0:256]
                        mmgrp(ps_, [
                            (wg[:, cc * 2048 + fc * 128:cc * 2048 + (fc + 1) * 128],
                             xcf[b][:, cc * 256:(cc + 1) * 256])
                            for cc in range(NCC)])
                        if fc < 8:
                            dst = sf[0][:, fc * 256:(fc + 1) * 256]
                        else:
                            dst = si[b][:, (fc - 8) * 256:(fc - 7) * 256]
                        p.add("scalar",
                              (lambda s, d=dst, ps2=ps_, fc=fc: s.activation(
                                  d, ps2, Act.Sigmoid, bias=bga[:, fc:fc + 1])),
                              reads=[ps_, bga[:, fc:fc + 1]], writes=[dst])

                def p2_alphabeta(tc):
                    b = tc % 2
                    for cc in range(NCC):
                        sl = slice(cc * 256, (cc + 1) * 256)
                        p.add("scalar",
                              (lambda s, sl=sl, cc=cc, b=b: s.activation(
                                  al[b][:, sl], sf[0][:, sl], Act.Exp,
                                  scale=aco[:, cc:cc + 1])),
                              reads=[sf[0][:, sl], aco[:, cc:cc + 1]],
                              writes=[al[b][:, sl]])
                    p.add("vector", (lambda v, b=b: v.tensor_tensor(
                        sf[0][:, :], al[b][:, :], al[b][:, :], Alu.mult)),
                          reads=[al[b][:, :]], writes=[sf[0][:, :]])
                    p.add("scalar",
                          (lambda s, b=b: s.activation(bt[b][:, :], sf[0][:, :], Act.Sqrt,
                                                       bias=1.000001, scale=-1.0)),
                          reads=[sf[0][:, :], cst[:, 1:2]], writes=[bt[b][:, :]])

                def p2_uscany(tc):
                    b = tc % 2
                    p.add("vector", (lambda v, b=b: v.tensor_tensor(
                        ub[:, :], bt[b][:, :], si[b][:, :], Alu.mult)),
                          reads=[bt[b][:, :], si[b][:, :]], writes=[ub[:, :]])
                    p.add("vector", (lambda v, b=b: v.tensor_tensor(
                        si[b][:, :], ub[:, :], xcf[b][:, :], Alu.mult)),
                          reads=[ub[:, :], xcf[b][:, :]], writes=[si[b][:, :]])
                    for cc in range(NCC):
                        sl = slice(cc * 256, (cc + 1) * 256)
                        if tc == 0:
                            ini = 0.0
                            rd = [al[b][:, sl], si[b][:, sl]]
                        else:
                            ini = hbuf[1 - b][:, cc * 256 + 255:cc * 256 + 256]
                            rd = [al[b][:, sl], si[b][:, sl], ini]
                        p.add("vector",
                              (lambda v, sl=sl, b=b, ini=ini: v.tensor_tensor_scan(
                                  hbuf[b][:, sl], al[b][:, sl], si[b][:, sl], ini,
                                  Alu.mult, Alu.add)),
                              reads=rd, writes=[hbuf[b][:, sl]])
                    p.add("vector", (lambda v, b=b: v.tensor_tensor(
                        yt[b][:, :], gel[b][:, :], hbuf[b][:, :], Alu.mult)),
                          reads=[gel[b][:, :], hbuf[b][:, :]], writes=[yt[b][:, :]])

                def p2_hawk(tc):
                    b = tc % 2
                    for tbi in range(2):
                        for ch in range(2):
                            g = 4 * tc + 2 * tbi + ch
                            ps_ = po[g % 4][:, :]
                            mmgrp(ps_, [
                                (yt[b][:, cc * 256 + tbi * 128:cc * 256 + (tbi + 1) * 128],
                                 wo[:, cc * 1024 + ch * 512:cc * 1024 + (ch + 1) * 512])
                                for cc in range(NCC)])

                def p2_resid(tc):
                    b = tc % 2
                    for tbi in range(2):
                        for ch in range(2):
                            g = 4 * tc + 2 * tbi + ch
                            ps_ = po[g % 4][:, :]
                            sl = slice(tbi * 1024 + ch * 512, tbi * 1024 + (ch + 1) * 512)
                            p.add("vector",
                                  (lambda v, sl=sl, b=b, ps2=ps_: v.tensor_tensor(
                                      x1t[b][:, sl], ps2, xt2[tc % 3][:, sl], Alu.add)),
                                  reads=[ps_, xt2[tc % 3][:, sl]], writes=[x1t[b][:, sl]])
                        t0 = tbi * 1024
                        p.add("vector", (lambda v, b=b, tbi=tbi, t0=t0: v.bn_stats(
                            bn2[:, 12 * tbi:12 * tbi + 6], x1t[b][:, t0:t0 + 512])),
                              reads=[x1t[b][:, t0:t0 + 512]],
                              writes=[bn2[:, 12 * tbi:12 * tbi + 6]])
                        p.add("vector", (lambda v, b=b, tbi=tbi, t0=t0: v.bn_stats(
                            bn2[:, 12 * tbi + 6:12 * tbi + 12],
                            x1t[b][:, t0 + 512:t0 + 1024])),
                              reads=[x1t[b][:, t0 + 512:t0 + 1024]],
                              writes=[bn2[:, 12 * tbi + 6:12 * tbi + 12]])
                        p.add("vector", (lambda v, tbi=tbi: v.bn_aggr(
                            mv2[:, 2 * tbi:2 * tbi + 2], bn2[:, 12 * tbi:12 * tbi + 12])),
                              reads=[bn2[:, 12 * tbi:12 * tbi + 12]],
                              writes=[mv2[:, 2 * tbi:2 * tbi + 2]])
                        p.add("scalar", (lambda s, tbi=tbi: s.activation(
                            sd2[:, tbi:tbi + 1], mv2[:, 2 * tbi + 1:2 * tbi + 2],
                            Act.Sqrt, bias=1e-5)),
                              reads=[mv2[:, 2 * tbi + 1:2 * tbi + 2], cst[:, 0:1]],
                              writes=[sd2[:, tbi:tbi + 1]])
                        p.add("vector", (lambda v, tbi=tbi: v.reciprocal(
                            ri2[:, tbi:tbi + 1], sd2[:, tbi:tbi + 1])),
                              reads=[sd2[:, tbi:tbi + 1]], writes=[ri2[:, tbi:tbi + 1]])
                        p.add("vector", (lambda v, b=b, tbi=tbi, t0=t0: v.tensor_scalar(
                            ln2t[b][:, t0:t0 + 1024], x1t[b][:, t0:t0 + 1024],
                            mv2[:, 2 * tbi:2 * tbi + 1], ri2[:, tbi:tbi + 1],
                            Alu.subtract, Alu.mult)),
                              reads=[x1t[b][:, t0:t0 + 1024], mv2[:, 2 * tbi:2 * tbi + 1],
                                     ri2[:, tbi:tbi + 1]],
                              writes=[ln2t[b][:, t0:t0 + 1024]])
                    for tbi in range(2):
                        r0 = tc * 256 + tbi * 128
                        dma("sync", sx1_d.ap()[r0:r0 + 128, :],
                            x1t[b][:, tbi * 1024:(tbi + 1) * 1024])
                        dma("sync", sxln2_d.ap()[r0:r0 + 128, :],
                            ln2t[b][:, tbi * 1024:(tbi + 1) * 1024])

                p2_loads_xa(0)
                p2_loads_gel(0)
                for tc in range(NP2):
                    p2_conv(tc)
                    if tc + 1 < NP2:
                        p2_loads_xa(tc + 1)
                    p2_gates(tc)
                    if tc >= 1:
                        p2_uscany(tc - 1)
                        if tc + 1 < NP2:
                            p2_loads_gel(tc + 1)
                        p2_hawk(tc - 1)
                    elif tc + 1 < NP2:
                        p2_loads_gel(tc + 1)
                    p2_alphabeta(tc)
                    if tc >= 1:
                        p2_resid(tc - 1)
                p2_uscany(NP2 - 1)
                p2_hawk(NP2 - 1)
                p2_resid(NP2 - 1)

        if ("p4" in phases) or ("p6" in phases):
            kfm = sb("kfm", (128, 4 * T), F16)
            vals = sb("vals", (128, NTB * NH * DH), BF16)
            mbias = sb("mbias", (128, NH))
        if "p4" in phases:
            with ExitStack() as pc:
                psb = lambda n, s, d=F32: pc.enter_context(nc.sbuf_tensor(n, list(s), d))
                xlnf2 = psb("p4xf", (128, NCC * T), F16)
                gm = psb("p4gm", (128, NTB * 64))
                pca = ExitStack()
                asb = lambda n, s, d=F32: pca.enter_context(nc.sbuf_tensor(n, list(s), d))
                wqs = asb("p4wq", (128, NCC * 512), F16)
                wks = asb("p4wk", (128, NCC * 512), F16)
                wgs = asb("p4wg", (128, NCC * 64), F16)
                ones = asb("p4on", (128, 32), F16)
                identf = asb("p4id", (128, 128))
                qst = [asb(f"p4q{i}", (128, 512), F16) for i in range(2)]
                sqt = [asb(f"p4s{i}", (128, 512), F16) for i in range(2)]
                slotb = asb("p4sl", (1, 128))
                hmax = asb("p4hm", (1, 16))
                G4 = asb("p4G", (128, 256))
                L4 = asb("p4L", (128, 256))
                aA1 = asb("p4a1", (128, 128))
                aA2 = asb("p4a2", (128, 64))
                aM8 = asb("p4a8", (128, 32))
                aF = asb("p4aF", (128, 256))
                aP = asb("p4aP", (128, 256))
                aPv = asb("p4aV", (128, 256))
                aD = asb("p4aD", (128, 256))
                aS = asb("p4aS", (128, 256))
                aone = asb("p4a1s", (128, 256))
                gmf = asb("p4gf", (64, T))
                mprod = asb("p4mp", (1, 8))
                mrt = asb("p4mr", (1, 8))
                mneg = asb("p4mn", (1, 8))
                pq = [pca.enter_context(nc.psum_tensor(f"p4pq{i}", [128, 512], F32))
                      for i in range(2)]
                pn = [pca.enter_context(nc.psum_tensor(f"p4pn{i}", [1, 512], F32))
                      for i in range(2)]
                pg4 = [pca.enter_context(nc.psum_tensor(f"p4pg{i}", [128, 512], F32))
                       for i in range(2)]
                pgt = pca.enter_context(nc.psum_tensor("p4pt", [64, 128], F32))
                p.block()
                for cc in range(NCC):
                    dma("sync", xlnf2[:, cc * T:(cc + 1) * T],
                        sxln2_d.ap()[:, cc * 128:(cc + 1) * 128], transpose=True)
                for cc in range(NCC):
                    dma("gpsimd_d", wqs[:, cc * 512:(cc + 1) * 512],
                        wq_d.ap()[cc * 128:(cc + 1) * 128, :])
                for cc in range(NCC):
                    dma("gpsimd_d", wks[:, cc * 512:(cc + 1) * 512],
                        wk_d.ap()[cc * 128:(cc + 1) * 128, :])
                for cc in range(NCC):
                    dma("gpsimd_d", wgs[:, cc * 64:(cc + 1) * 64],
                        wgate_d.ap()[cc * 128:(cc + 1) * 128, :])
                dma("gpsimd_d", ones[:, :], ones_d.ap())
                dma("gpsimd_d", identf[:, :], ident_d.ap())
                p.add("vector", lambda v: v.memset(aone[:, :], 1.0), writes=[aone[:, :]])
                p.add("vector", lambda v: v.memset(aPv[:, 0:8], 0.0), writes=[aPv[:, 0:8]])

                def p4_norm_mms(j):
                    for hhh in range(2):
                        nidx = 2 * j + hhh
                        ps_ = pn[nidx % 2][:, :]
                        p.add("tensor",
                              (lambda t, hhh=hhh, j=j, ps2=ps_: t.matmul(
                                  ps2, ones[hhh * 64:(hhh + 1) * 64, 0:1],
                                  sqt[j % 2][hhh * 64:(hhh + 1) * 64, :],
                                  start=True, stop=True,
                                  tile_position=(hhh * 64, 0))),
                              reads=[ones[hhh * 64:(hhh + 1) * 64, 0:1],
                                     sqt[j % 2][hhh * 64:(hhh + 1) * 64, :]],
                              writes=[ps_])
                        dst = slotb[0:1, nidx:nidx + 1]
                        p.add("vector",
                              (lambda v, d=dst, ps2=ps_: v.tensor_reduce(
                                  d, ps2, AX.X, Alu.max)),
                              reads=[ps_], writes=[dst])

                for idx in range(64):
                    w = wqs if idx < 32 else wks
                    hp = (idx % 32) // 8
                    tg = idx % 8
                    ps_ = pq[idx % 2][:, :]
                    mmgrp(ps_, [
                        (w[:, cc * 512 + hp * 128:cc * 512 + (hp + 1) * 128],
                         xlnf2[:, cc * T + tg * 512:cc * T + (tg + 1) * 512])
                        for cc in range(NCC)])
                    if idx < 32:
                        dst = qst[idx % 2][:, :]
                        p.add("scalar",
                              (lambda s, d=dst, ps2=ps_: s.activation(
                                  d, ps2, Act.Copy, scale=0.125)),
                              reads=[ps_], writes=[dst])
                        dma("sync", sq_d.ap()[hp, :, tg * 512:(tg + 1) * 512], dst)
                    else:
                        dst = kfm[:, hp * T + tg * 512:hp * T + (tg + 1) * 512]
                        p.add("scalar",
                              (lambda s, d=dst, ps2=ps_: s.copy(d, ps2)),
                              reads=[ps_], writes=[dst])
                    sdst = sqt[idx % 2][:, :]
                    p.add("scalar",
                          (lambda s, d=sdst, ps2=ps_: s.activation(d, ps2, Act.Square)),
                          reads=[ps_], writes=[sdst])
                    if idx >= 1:
                        p4_norm_mms(idx - 1)
                p4_norm_mms(63)
                for side in range(2):
                    for hp in range(4):
                        for hhh in range(2):
                            h = hp * 2 + hhh
                            base = side * 64 + hp * 16 + hhh
                            src = bass.AP(slotb.tensor if hasattr(slotb, "tensor")
                                          else None, 0, []) if False else None
                            sl = slotb[0:1, base:base + 15:2]
                            dst = hmax[0:1, side * 8 + h:side * 8 + h + 1]
                            p.add("vector",
                                  (lambda v, d=dst, s_=sl: v.tensor_reduce(
                                      d, s_, AX.X, Alu.max)),
                                  reads=[sl], writes=[dst])
                p.add("vector", lambda v: v.tensor_tensor(
                    mprod[:, :], hmax[0:1, 0:8], hmax[0:1, 8:16], Alu.mult),
                      reads=[hmax[:, :]], writes=[mprod[:, :]])
                p.add("scalar", lambda s: s.activation(mrt[:, :], mprod[:, :], Act.Sqrt),
                      reads=[mprod[:, :]], writes=[mrt[:, :]])
                p.add("vector", lambda v: v.tensor_scalar(
                    mneg[:, :], mrt[:, :], -0.125, None, Alu.mult),
                      reads=[mrt[:, :]], writes=[mneg[:, :]])
                dma("sync", sms_d.ap()[0:1, :], mneg[0:1, :])
                dma("sync", mbias[:, :], bass.AP(sms_d, 0, [[0, 128], [1, NH]]))

                for tb in range(NTB):
                    ps_ = pg4[tb % 2][:, 0:64]
                    mmgrp(ps_, [
                        (xlnf2[:, cc * T + tb * 128:cc * T + (tb + 1) * 128],
                         wgs[:, cc * 64:(cc + 1) * 64])
                        for cc in range(NCC)])
                    dst = G4[:, (tb % 4) * 64:(tb % 4 + 1) * 64]
                    p.add("scalar",
                          (lambda s, d=dst, ps2=ps_: s.activation(d, ps2, Act.Sigmoid)),
                          reads=[ps_], writes=[dst])
                    ldst = L4[:, (tb % 4) * 64:(tb % 4 + 1) * 64]
                    p.add("scalar",
                          (lambda s, d=ldst, ps2=ps_: s.copy(d, ps2)),
                          reads=[ps_], writes=[ldst])
                    if tb % 4 == 3:
                        tb0 = tb - 3
                        p.add("vector", lambda v: v.tensor_tensor(
                            aA1[:, :], L4[:, 0:256:2], L4[:, 1:256:2], Alu.max),
                              reads=[L4[:, :]], writes=[aA1[:, :]])
                        p.add("vector", lambda v: v.tensor_tensor(
                            aA2[:, :], aA1[:, 0:128:2], aA1[:, 1:128:2], Alu.max),
                              reads=[aA1[:, :]], writes=[aA2[:, :]])
                        p.add("vector", lambda v: v.tensor_tensor(
                            aM8[:, :], aA2[:, 0:64:2], aA2[:, 1:64:2], Alu.max),
                              reads=[aA2[:, :]], writes=[aM8[:, :]])
                        mb_ = aM8[:, :].unsqueeze(2).broadcast_to([128, 32, 8])
                        p.add("vector", (lambda v, mb_=mb_: v.tensor_tensor(
                            aF[:, :].rearrange("p (a c) -> p a c", a=32),
                            L4[:, :].rearrange("p (a c) -> p a c", a=32),
                            mb_, Alu.is_ge)),
                              reads=[L4[:, :], aM8[:, :]], writes=[aF[:, :]])
                        p.add("vector", lambda v: v.tensor_tensor_scan(
                            aP[:, :], aone[:, :], aF[:, :], 0.0, Alu.mult, Alu.add),
                              reads=[aone[:, :], aF[:, :]], writes=[aP[:, :]])
                        pb_ = aP[:, 7:255:8].unsqueeze(2).broadcast_to([128, 31, 8])
                        p.add("vector", (lambda v, pb_=pb_: v.tensor_copy(
                            aPv[:, 8:256].rearrange("p (a c) -> p a c", a=31), pb_)),
                              reads=[aP[:, :]], writes=[aPv[:, 8:256]])
                        p.add("vector", lambda v: v.tensor_tensor(
                            aD[:, :], aP[:, :], aPv[:, :], Alu.subtract),
                              reads=[aP[:, :], aPv[:, :]], writes=[aD[:, :]])
                        p.add("vector", lambda v: v.tensor_scalar(
                            aS[:, :], aD[:, :], 1.0, None, Alu.is_equal),
                              reads=[aD[:, :]], writes=[aS[:, :]])
                        p.add("vector", lambda v: v.tensor_tensor(
                            aD[:, :], aS[:, :], aF[:, :], Alu.mult),
                              reads=[aS[:, :], aF[:, :]], writes=[aD[:, :]])
                        p.add("vector", (lambda v, tb0=tb0: v.tensor_tensor(
                            gm[:, tb0 * 64:(tb0 + 4) * 64], aD[:, :],
                            G4[:, :], Alu.mult)),
                              reads=[aD[:, :], G4[:, :]],
                              writes=[gm[:, tb0 * 64:(tb0 + 4) * 64]])
                for tb in range(NTB):
                    p.add("tensor", (lambda t, tb=tb: t.transpose(
                        pgt[:, :], gm[:, tb * 64:(tb + 1) * 64], identf[:, :])),
                          reads=[gm[:, tb * 64:(tb + 1) * 64], identf[:, :]],
                          writes=[pgt[:, :]])
                    dst = gmf[:, tb * 128:(tb + 1) * 128]
                    p.add("scalar", (lambda s, d=dst: s.copy(d, pgt[:, :])),
                          reads=[pgt[:, :]], writes=[dst])
                dma("sync", sgmf_d.ap()[:, :], gmf[:, :])

                pca.close()
                p.block()
                wvs = [psb(f"p4wv{i}", (128, NCC * 512), F16) for i in range(2)]
                vQ = psb("p4vQ", (128, 2048))
                pvb2 = [pc.enter_context(nc.psum_tensor(f"p4pv{i}", [128, 2048], F32))
                        for i in range(2)]
                for h in range(NH):
                    for cc in range(NCC):
                        src = bass.AP(wv_d, cc * 128 * (NH * NE * DH) + h * (NE * DH),
                                      [[NH * NE * DH, 128], [1, NE * DH]])
                        dma("gpsimd_d", wvs[h % 2][:, cc * 512:(cc + 1) * 512], src)
                    for tb in range(NTB):
                        vidx = h * NTB + tb
                        pvb = pvb2[(vidx // 4) % 2]
                        ps_ = pvb[:, (tb % 4) * 512:(tb % 4 + 1) * 512]
                        mmgrp(ps_, [
                            (xlnf2[:, cc * T + tb * 128:cc * T + (tb + 1) * 128],
                             wvs[h % 2][:, cc * 512:(cc + 1) * 512])
                            for cc in range(NCC)])
                        if tb % 4 != 3:
                            continue
                        tb0 = tb - 3
                        bb = bass.AP(gm, tb0 * 64 + h * 8,
                                     [[NTB * 64, 128], [64, 4], [1, 8], [0, 64]])
                        psall = pvb[:, :]
                        p.add("vector", (lambda v, bb=bb, pvb=pvb: v.tensor_tensor(
                            vQ[:, :].rearrange("p (t e d) -> p t e d", t=4, e=8),
                            pvb[:, :].rearrange("p (t e d) -> p t e d", t=4, e=8),
                            bb, Alu.mult)),
                              reads=[psall,
                                     gm[:, tb0 * 64:(tb0 + 4) * 64]],
                              writes=[vQ[:, :]])
                        vdst = bass.AP(vals, (tb0 * NH + h) * 64,
                                       [[NTB * NH * 64, 128], [NH * 64, 4], [1, 64]])
                        vsrc = vQ[:, :].rearrange("p (t e d) -> p t d e", t=4, e=8)

                        def _vred(v, d=vdst, s_=vsrc):
                            with nc.allow_low_precision("fp32-internal reduce, bf16 out"):
                                return v.tensor_reduce(d, s_, AX.X, Alu.add)
                        p.add("vector", _vred, reads=[vQ[:, :]],
                              writes=[vals[:, tb0 * NH * 64:(tb0 + 4) * NH * 64]])

        if "p6" in phases:
            with ExitStack() as pc:
                psb = lambda n, s, d=F32: pc.enter_context(nc.sbuf_tensor(n, list(s), d))
                woes = psb("p6wo", (128, NE * 4 * 1024), F16)
                tri = psb("p6tr", (128, 4 * 512), BF16)
                ones32b = psb("p6on1", (128, 32), BF16)
                qg_t = [psb(f"p6q{i}", (128, 4 * 512), F16) for i in range(2)]
                at = [psb(f"p6a{i}", (128, 512), BF16) for i in range(16)]
                rs8a_t = psb("p6ra", (128, 1024))
                rs8a = [rs8a_t[:, 0:512], rs8a_t[:, 512:1024]]
                rsx = [psb(f"p6rx{i}", (128, 512)) for i in range(4)]
                rsz = psb("p6rz", (128, 1024))
                onorm = psb("p6on", (128, 4 * 512), F16)
                gmx = [psb("p6gx0", (128, 4 * 2048), F16)]
                msk = [psb(f"p6m{i}", (128, 512), F16) for i in range(2)]
                x1b = [psb(f"p6x{i}", (128, 1024)) for i in range(2)]
                ps8 = [pc.enter_context(nc.psum_tensor(f"p6p{i}", [128, 512], F32))
                       for i in range(8)]
                p.block()
                for e in range(NE):
                    for hp in range(4):
                        dma("gpsimd_d",
                            woes[:, (e * 4 + hp) * 1024:(e * 4 + hp + 1) * 1024],
                            woe_d.ap()[e, hp, :, :])
                dma("gpsimd_d", tri[:, :].rearrange("p (a c) -> p a c", a=4),
                    bass.AP(tri_d, 0, [[512, 128], [128 * 512, 4], [1, 512]]))
                dma("gpsimd_d", ones32b[:, :], ones_d.ap())

                for hp in range(4):
                    dma("sync", qg_t[0][:, hp * 512:(hp + 1) * 512],
                        sq_d.ap()[hp, :, 0:512])
                def gmx_load(eh, g):
                    gb = gmx[0]
                    for hp in range(4):
                        for hh in range(2):
                            h = 2 * hp + hh
                            src = bass.AP(
                                sgmf_d, (h * 8 + eh * 4) * T + g * 512,
                                [[0, 64], [T, 4], [1, 512]])
                            dma("gpsimd_d",
                                gb[hh * 64:(hh + 1) * 64,
                                   hp * 2048:(hp + 1) * 2048]
                                .rearrange("p (a c) -> p a c", a=4), src)

                eorder = [1, 2, 0, 3]

                def p6_chain_head(g):
                    p.add("scalar", lambda s: s.activation(
                        rsz[:, :], rs8a_t[:, :], Act.Ln),
                          reads=[rs8a_t[:, :]], writes=[rsz[:, :]])
                    p.add("scalar", lambda s: s.activation(
                        rs8a_t[:, :], rsz[:, :], Act.Exp, scale=-1.0),
                          reads=[rsz[:, :]], writes=[rs8a_t[:, :]])
                    for qd in range(2):
                        dma("sync", srs_d.ap()[qd * 4:(qd + 1) * 4,
                                               g * 512:(g + 1) * 512],
                            bass.AP(rs8a_t, qd * 512, [[1024 * 32, 4], [1, 512]]))
                    for hp in range(4):
                        for hh in range(2):
                            h = 2 * hp + hh
                            src = bass.AP(srs_d, h * T + g * 512,
                                          [[0, 64], [1, 512]])
                            dma("sync", rsx[hp][hh * 64:(hh + 1) * 64, :], src)
                    for hp in range(4):
                        p.add("vector", (lambda v, hp=hp: v.tensor_tensor(
                            onorm[:, hp * 512:(hp + 1) * 512], ps8[2 + hp][:, :],
                            rsx[hp][:, :], Alu.mult)),
                              reads=[ps8[2 + hp][:, :], rsx[hp][:, :]],
                              writes=[onorm[:, hp * 512:(hp + 1) * 512]])
                    if g + 1 < 8:
                        for qd in range(2):
                            p.add("vector",
                                  (lambda v, qd=qd: v.memset(rs8a[qd][:, :], 0.0)),
                                  writes=[rs8a[qd][:, :]])
                    for i2 in range(2):
                        r0 = g * 512 + eorder[i2] * 128
                        dma("gpsimd_d", x1b[i2 % 2][:, :], sx1_d.ap()[r0:r0 + 128, :])

                def p6_chain_tail(g):
                    for eh in range(2):
                        gb = gmx[0]
                        if eh == 1:
                            gmx_load(1, g)
                        for e4 in range(4):
                            e = eh * 4 + e4
                            for hp in range(4):
                                mb = msk[(e4 * 4 + hp) % 2]
                                gsl = gb[:, hp * 2048 + e4 * 512:hp * 2048 + (e4 + 1) * 512]
                                osl = onorm[:, hp * 512:(hp + 1) * 512]
                                p.add("vector", (lambda v, mb=mb, gsl=gsl, osl=osl:
                                                 v.tensor_tensor(
                                    mb[:, :], osl, gsl, Alu.mult)),
                                      reads=[osl, gsl], writes=[mb[:, :]])
                                first = (eh == 0 and e4 == 0 and hp == 0)
                                last = (eh == 1 and e4 == 3 and hp == 3)
                                def o2(t, mb=mb, e=e, hp=hp, first=first, last=last):
                                    for qb in range(4):
                                        for ch in range(2):
                                            ins = t.matmul(
                                                ps8[qb * 2 + ch][:, :],
                                                mb[:, qb * 128:(qb + 1) * 128],
                                                woes[:, (e * 4 + hp) * 1024 + ch * 512:
                                                     (e * 4 + hp) * 1024 + (ch + 1) * 512],
                                                start=first, stop=last)
                                    return ins
                                p.add("tensor", o2,
                                      reads=[mb[:, :],
                                             woes[:, (e * 4 + hp) * 1024:
                                                  (e * 4 + hp + 1) * 1024]],
                                      writes=[ps8[i][:, :] for i in range(8)])
                    for i2, qb in enumerate(eorder):
                        bo = i2 % 2
                        r0 = g * 512 + qb * 128
                        for ch in range(2):
                            p.add("vector", (lambda v, bo=bo, qb=qb, ch=ch:
                                             v.tensor_tensor(
                                x1b[bo][:, ch * 512:(ch + 1) * 512],
                                ps8[qb * 2 + ch][:, :],
                                x1b[bo][:, ch * 512:(ch + 1) * 512], Alu.add)),
                                  reads=[ps8[qb * 2 + ch][:, :],
                                         x1b[bo][:, ch * 512:(ch + 1) * 512]],
                                  writes=[x1b[bo][:, ch * 512:(ch + 1) * 512]])
                        dma("scalar_d", y_d.ap()[r0:r0 + 128, :], x1b[bo][:, :])
                        if i2 + 2 < 4:
                            rn = g * 512 + eorder[i2 + 2] * 128
                            dma("gpsimd_d", x1b[(i2 + 2) % 2][:, :],
                                sx1_d.ap()[rn:rn + 128, :])

                for qd in range(2):
                    p.add("vector", (lambda v, qd=qd: v.memset(rs8a[qd][:, :], 0.0)),
                          writes=[rs8a[qd][:, :]])
                for qg in range(8):
                    nkb = 4 * (qg + 1)
                    bq = qg % 2
                    if qg == 0:
                        gmx_load(0, 0)
                    pool = [ps8[0], ps8[1], ps8[6], ps8[7]]
                    pctr = [0]

                    def p6_sims(kb, hs=range(NH)):
                        for h in hs:
                            hp, hh = h // 2, h % 2
                            sl = at[(kb * 8 + h) % 16]
                            pss = pool[pctr[0] % 4][:, :]
                            pctr[0] += 1
                            lhs = kfm[hh * 64:(hh + 1) * 64,
                                      hp * T + kb * 128:hp * T + (kb + 1) * 128]
                            rhs = qg_t[bq][hh * 64:(hh + 1) * 64,
                                           hp * 512:(hp + 1) * 512]
                            p.add("tensor",
                                  (lambda t, ps2=pss, l=lhs, r=rhs, hh=hh: t.matmul(
                                      ps2, l, r, start=True, stop=True,
                                      tile_position=(hh * 64, 0))),
                                  reads=[lhs, rhs], writes=[pss])
                            p.add("scalar",
                                  (lambda s, d=sl, ps2=pss, h=h: s.activation(
                                      d[:, :], ps2, Act.Exp,
                                      bias=mbias[:, h:h + 1])),
                                  reads=[pss, mbias[:, h:h + 1]], writes=[sl[:, :]])
                            if kb >= 4 * qg:
                                j = kb - 4 * qg
                                p.add("vector",
                                      (lambda v, d=sl, j=j: v.tensor_tensor(
                                          d[:, :], d[:, :],
                                          tri[:, j * 512:(j + 1) * 512], Alu.mult)),
                                      reads=[sl[:, :], tri[:, j * 512:(j + 1) * 512]],
                                      writes=[sl[:, :]])

                    def p6_consume_av(kb, hps):
                        first = (kb == 0)
                        last = (kb == nkb - 1)
                        for hp in hps:
                            pss = ps8[2 + hp]
                            a0 = at[(kb * 8 + 2 * hp) % 16]
                            a1 = at[(kb * 8 + 2 * hp + 1) % 16]
                            v0 = vals[:, (kb * NH + 2 * hp) * 64:(kb * NH + 2 * hp + 1) * 64]
                            v1 = vals[:, (kb * NH + 2 * hp + 1) * 64:(kb * NH + 2 * hp + 2) * 64]
                            def av(t, pss=pss, a0=a0, a1=a1, v0=v0, v1=v1,
                                   first=first, last=last):
                                t.matmul(pss[0:64, :], v0, a0[:, :],
                                         start=first, stop=last,
                                         tile_position=(0, 0),
                                         skip_group_check=True)
                                return t.matmul(pss[64:128, :], v1, a1[:, :],
                                               start=first, stop=last,
                                               tile_position=(0, 64),
                                               skip_group_check=True)
                            p.add("tensor", av,
                                  reads=[a0[:, :], a1[:, :], v0, v1],
                                  writes=[pss[:, :]])

                    def p6_consume_sums(kb):
                        for qd in range(2):
                            pss = pool[pctr[0] % 4]
                            pctr[0] += 1
                            ats = [at[(kb * 8 + qd * 4 + j) % 16] for j in range(4)]
                            def sm(t, pss=pss, ats=ats):
                                for j in range(4):
                                    ins = t.matmul(pss[32 * j:32 * j + 32, :],
                                                   ones32b[:, :], ats[j][:, :],
                                                   start=True, stop=True,
                                                   tile_position=(0, 32 * j),
                                                   skip_group_check=True)
                                return ins
                            p.add("tensor", sm,
                                  reads=[a[:, :] for a in ats] + [ones32b[:, :]],
                                  writes=[pss[:, :]])
                            racc = rs8a[qd][:, :]
                            p.add("vector",
                                  (lambda v, r=racc, ps2=pss: v.tensor_tensor(
                                      r, ps2[:, :], r, Alu.add)),
                                  reads=[pss[:, :], racc], writes=[racc])

                    if qg >= 1:
                        p6_chain_head(qg - 1)
                    def p6_kb(kb):
                        if kb == 1 and qg >= 1:
                            p6_sims(kb)
                            p6_chain_tail(qg - 1)
                            gmx_load(0, qg)
                            p6_consume_av(kb - 1, (0, 1, 2, 3))
                            p6_consume_sums(kb - 1)
                        elif kb >= 1:
                            p6_sims(kb, range(0, 4))
                            p6_consume_av(kb - 1, (0, 1))
                            p6_sims(kb, range(4, 8))
                            p6_consume_av(kb - 1, (2, 3))
                            p6_consume_sums(kb - 1)
                        else:
                            p6_sims(kb)

                    for kb in range(nkb):
                        p6_kb(kb)
                        if kb == 2 and qg + 1 < 8:
                            for hp in range(4):
                                dma("sync", qg_t[(qg + 1) % 2]
                                    [:, hp * 512:(hp + 1) * 512],
                                    sq_d.ap()[hp, :, (qg + 1) * 512:(qg + 2) * 512])
                    p6_consume_av(nkb - 1, (0, 1, 2, 3))
                    p6_consume_sums(nkb - 1)
                p6_chain_head(7)
                p6_chain_tail(7)

        p.emit()
    return nc


def _host_inputs(inputs):
    g = lambda k: np.asarray(inputs[k], dtype=np.float32)
    x = np.ascontiguousarray(g("L0_attn_x"))
    com = {
        "winT": np.ascontiguousarray(g("L0_attn_w_in").T),
        "wgT": np.ascontiguousarray(g("L0_attn_w_gates").T),
        "woT": np.ascontiguousarray(g("L0_attn_w_out").T),
        "wcv": np.ascontiguousarray(
            g("L0_attn_w_conv").reshape(NCC, 128, 4).transpose(1, 0, 2)),
        "bcv": np.ascontiguousarray(g("L0_attn_b_conv").reshape(NCC, 128).T),
        "bga": np.ascontiguousarray(g("L0_attn_b_gates").reshape(16, 128).T),
        "fb": np.ascontiguousarray(g("L0_attn_forget_base").reshape(NCC, 128).T),
        "wq": np.ascontiguousarray(g("L0_ffn_w_q")),
        "wk": np.ascontiguousarray(g("L0_ffn_w_k")),
        "wgate": np.ascontiguousarray(g("L0_ffn_w_gate")),
        "wv": np.ascontiguousarray(
            np.transpose(g("L0_ffn_value_experts"), (2, 1, 0, 3)).reshape(D, NH * NE * DH)),
        "woe": np.ascontiguousarray(
            g("L0_ffn_output_experts").reshape(NE, 4, 2, DH, D).reshape(NE, 4, 128, D)),
        "c_ident": np.eye(128, dtype=np.float32),
        "c_ones": np.ones((128, 32), dtype=np.float32),
        "c_tri": np.stack([
            (np.arange(512)[None, :] >= (j * 128 + np.arange(128))[:, None])
            .astype(np.float32) for j in range(4)]),
    }
    return x, com


def kernel(**inputs):
    x, com = _host_inputs(inputs)
    nc = _build()
    in_maps = [dict(com, x=np.ascontiguousarray(x[b])) for b in range(8)]
    res = run_bass_kernel_spmd(nc, in_maps, core_ids=list(range(8)))
    return np.stack([r["y"] for r in res.results], axis=0)


# version 27
# speedup vs baseline: 1.0230x; 1.0230x over previous
import numpy as np
from contextlib import ExitStack

import concourse.bass as bass
import concourse.mybir as mybir
from concourse.bass_utils import run_bass_kernel_spmd

F32 = mybir.dt.float32
F16 = mybir.dt.float16
BF16 = mybir.dt.bfloat16
U32 = mybir.dt.uint32
Alu = mybir.AluOpType
Act = mybir.ActivationFunctionType
AX = mybir.AxisListType

T = 4096
D = 1024
NCC = 8
NTB = 32
NH = 8
DH = 64
NE = 8
P2TC = 256
NP2 = T // P2TC
ALL_PHASES = ("p0", "p1", "p2", "p4", "p6")

_DMA_KINDS = ("sync", "gpsimd_d", "scalar_d")
_ENGINE_OF = {"tensor": "tensor", "vector": "vector", "scalar": "scalar",
              "sync": "sync", "gpsimd_d": "gpsimd", "scalar_d": "scalar"}


def _region(ap):
    t = ap.tensor
    name = t.name
    apl = [list(x) for x in ap.ap]
    off = int(ap.offset)
    if t.__class__.__name__.startswith("DRam"):
        L = int(list(t.shape)[-1])
        r0, c0 = off // L, off % L
        rext = cext = 0
        ok = True
        for s, c in apl:
            s = abs(s)
            if s == 0 or c == 1:
                continue
            if s % L == 0:
                rext += (s // L) * (c - 1)
            elif s < L:
                cext += s * (c - 1)
            else:
                ok = False
        if ok and c0 + cext < L:
            return (name, r0, r0 + rext + 1, c0, c0 + cext + 1)
        return (name, 0, 1 << 60, 0, 1 << 60)
    import math
    pitch = 1
    for d in list(t.shape)[1:]:
        pitch *= int(d)
    p0 = off // pitch
    c0 = off % pitch
    pcnt = apl[0][1] if apl else 1
    pstep = apl[0][0] if apl else pitch
    if pstep % pitch == 0:
        pext = (pstep // pitch) * (pcnt - 1) + 1
    else:
        p0, pext = 0, 128
    cspan = sum(abs(s) * (c - 1) for s, c in apl[1:]) + 1
    return (name, p0, p0 + pext, c0, c0 + cspan)


def _overlap(a, b):
    return (a[0] == b[0] and a[1] < b[2] and b[1] < a[2]
            and a[3] < b[4] and b[3] < a[4])


class _Op:
    __slots__ = ("kind", "emit", "reads", "writes", "idx", "inc", "waits", "drain")

    def __init__(self, kind, emit, reads, writes, idx):
        self.kind = kind
        self.emit = emit
        self.reads = reads
        self.writes = writes
        self.idx = idx
        self.inc = False
        self.waits = {}
        self.drain = False


NPOOL = 12


class Prog:
    def __init__(self, nc, ctx):
        self.nc = nc
        self.sems = {k: ctx.enter_context(nc.semaphore(f"as_{k}"))
                     for k in _ENGINE_OF if k not in _DMA_KINDS}
        self.dsems = {k: [ctx.enter_context(nc.semaphore(f"ad_{k}{i}"))
                          for i in range(NPOOL)] for k in _DMA_KINDS}
        self.counts = {k: 0 for k in _ENGINE_OF}
        self.blocks = []
        self.cur = None

    def block(self):
        self.cur = []
        self.blocks.append(self.cur)

    def add(self, kind, emit, reads=(), writes=()):
        op = _Op(kind, emit,
                 [_region(a) for a in reads],
                 [_region(a) for a in writes],
                 self.counts[kind])
        self.counts[kind] += 1
        self.cur.append(op)
        return op

    def _analyze(self):
        for ops in self.blocks:
            writers = {}
            readers = {}
            last_drain = {}
            pos_of = {id(o): i for i, o in enumerate(ops)}
            for i, op in enumerate(ops):
                deps = []
                for r in op.reads:
                    for reg, w in writers.get(r[0], ()):
                        if _overlap(r, reg):
                            deps.append(w)
                for wr in op.writes:
                    for reg, w in writers.get(wr[0], ()):
                        if _overlap(wr, reg):
                            deps.append(w)
                    for reg, rd in readers.get(wr[0], ()):
                        if _overlap(wr, reg):
                            deps.append(rd)
                need_drain = False
                for d in deps:
                    if d is op:
                        continue
                    if d.kind == op.kind and d.kind not in _DMA_KINDS:
                        if op.kind in ("vector", "scalar"):
                            dp = pos_of[id(d)]
                            if dp >= last_drain.get(op.kind, 0):
                                need_drain = True
                    else:
                        key = (d.kind, d.idx % NPOOL) if d.kind in _DMA_KINDS \
                            else (d.kind, 0)
                        prev = op.waits.get(key, -1)
                        if d.idx > prev:
                            op.waits[key] = d.idx
                        d.inc = True
                if need_drain:
                    op.drain = True
                    last_drain[op.kind] = i
                for wr in op.writes:
                    writers.setdefault(wr[0], []).append((wr, op))
                for r in op.reads:
                    readers.setdefault(r[0], []).append((r, op))
        for ops in self.blocks:
            for op in ops:
                if op.kind in _DMA_KINDS:
                    op.inc = True
        self.rank = {k: {} for k in _ENGINE_OF if k not in _DMA_KINDS}
        inc_ops = {k: [] for k in self.rank}
        for ops in self.blocks:
            for op in ops:
                if op.inc and op.kind in self.rank:
                    inc_ops[op.kind].append(op.idx)
        for k, lst in inc_ops.items():
            lst.sort()
            for r, idx in enumerate(lst):
                self.rank[k][idx] = r + 1

    def _check_acyclic(self):
        for ops in self.blocks:
            n = len(ops)
            pos_of = {}
            by_kind = {}
            for i, op in enumerate(ops):
                pos_of[id(op)] = i
                by_kind.setdefault(op.kind, []).append(i)
            adj = [[] for _ in range(n)]
            indeg = [0] * n
            for lst in by_kind.values():
                for a, b in zip(lst, lst[1:]):
                    adj[a].append(b)
                    indeg[b] += 1
            idx2pos = {}
            for i, op in enumerate(ops):
                idx2pos[(op.kind, op.idx)] = i
            for i, op in enumerate(ops):
                for (k, _sl), tgt in op.waits.items():
                    j = idx2pos.get((k, tgt))
                    if j is not None:
                        adj[j].append(i)
                        indeg[i] += 1
            from collections import deque
            q = deque(i for i in range(n) if indeg[i] == 0)
            seen = 0
            while q:
                u = q.popleft()
                seen += 1
                for v in adj[u]:
                    indeg[v] -= 1
                    if indeg[v] == 0:
                        q.append(v)
            if seen != n:
                stuck = [(i, ops[i].kind, ops[i].idx, ops[i].waits)
                         for i in range(n) if indeg[i] > 0][:8]
                raise RuntimeError(f"cycle in schedule: {stuck}")

    def emit(self):
        self._analyze()
        self._check_acyclic()
        dma_done = {k: 0 for k in _DMA_KINDS}
        for ops in self.blocks:
            kinds = []
            for op in ops:
                if op.kind not in kinds:
                    kinds.append(op.kind)
            fence = dict(dma_done)
            eng_kinds = {}
            for k in kinds:
                eng_kinds.setdefault(_ENGINE_OF[k], []).append(k)
            with self.nc.Block() as blk:
                for engname, ks in eng_kinds.items():
                    kops = [o for o in ops if o.kind in ks]
                    getattr(blk, engname)(self._body(ks[0], kops, fence))
            for op in ops:
                if op.kind in _DMA_KINDS:
                    dma_done[op.kind] += 1
        with self.nc.Block() as blk:
            for k in _DMA_KINDS:
                total = self.counts[k]
                if total:
                    def mk(kk, tt):
                        def body(eng):
                            for sl in range(NPOOL):
                                n = tt // NPOOL + (1 if sl < tt % NPOOL else 0)
                                if n:
                                    eng.wait_ge(self.dsems[kk][sl], 16 * n)
                        return body
                    getattr(blk, _ENGINE_OF[k])(mk(k, total))

    def _body(self, kind, kops, fence):
        def body(eng):
            seen = {}
            def w(sem_h, val, key):
                if seen.get(key, -1) >= val:
                    return
                seen[key] = val
                eng.wait_ge(sem_h, val)
            for fk, cnt in fence.items():
                for sl in range(NPOOL):
                    n = cnt // NPOOL + (1 if sl < cnt % NPOOL else 0)
                    if n:
                        w(self.dsems[fk][sl], 16 * n, (fk, sl))
            for op in kops:
                for (wk, _sl), tgt in sorted(op.waits.items()):
                    if wk in _DMA_KINDS:
                        w(self.dsems[wk][tgt % NPOOL],
                          16 * (tgt // NPOOL + 1), (wk, tgt % NPOOL))
                    else:
                        r = self.rank[wk].get(tgt)
                        assert r is not None, (wk, tgt)
                        w(self.sems[wk], r, (wk, 0))
                if op.kind in _DMA_KINDS and op.idx >= NPOOL:
                    w(self.dsems[op.kind][op.idx % NPOOL],
                      16 * (op.idx // NPOOL), (op.kind, op.idx % NPOOL))
                if op.drain:
                    eng.drain()
                ins = op.emit(eng)
                if op.inc:
                    if op.kind in _DMA_KINDS:
                        ins.then_inc(self.dsems[op.kind][op.idx % NPOOL], 16)
                    else:
                        ins.then_inc(self.sems[op.kind], 1)
        return body


def _build(phases=ALL_PHASES, dbg=False):
    nc = bass.Bass("TRN2", target_bir_lowering=False)

    def din(name, shape, dt=F32):
        return nc.dram_tensor(name, list(shape), dt, kind="ExternalInput")

    x_d = din("x", (T, D))
    winT_d = din("winT", (D, 2 * D))
    wgT_d = din("wgT", (D, 2 * D))
    woT_d = din("woT", (D, D))
    wcv_d = din("wcv", (128, NCC, 4))
    bcv_d = din("bcv", (128, NCC))
    bga_d = din("bga", (128, 16))
    fb_d = din("fb", (128, NCC))
    wq_d = din("wq", (D, NH * DH))
    wk_d = din("wk", (D, NH * DH))
    wgate_d = din("wgate", (D, NH * NE))
    wv_d = din("wv", (D, NH * NE * DH))
    woe_d = din("woe", (NE, 4, 128, D))
    ident_d = din("c_ident", (128, 128))
    ones_d = din("c_ones", (128, 32))
    tri_d = din("c_tri", (4, 128, 512))

    y_d = nc.dram_tensor("y", [T, D], F32, kind="ExternalOutput")

    def scr(name, shape, dt, prod, cons):
        if prod in phases:
            kind = "ExternalOutput" if dbg else "Internal"
        elif any(c in phases for c in cons):
            kind = "ExternalInput"
        else:
            kind = "Internal"
        return nc.dram_tensor(name, list(shape), dt, kind=kind)

    sxln1_d = scr("s_xln1", (T, D), F16, "p0", ("p1",))
    sgelu_d = scr("s_gelu", (NCC, 128, T), F16, "p1", ("p2",))
    sxa_d = scr("s_xa", (NCC, 128, T), F16, "p1", ("p2",))
    sx1_d = scr("s_x1", (T, D), F32, "p2", ("p6",))
    sxln2_d = scr("s_xln2", (T, D), F16, "p2", ("p4",))
    sq_d = scr("s_q", (4, 128, T), F16, "p4", ("p6",))
    sgmf_d = scr("s_gmf", (NH * NE, T), F32, "p4", ("p6",))
    sms_d = scr("s_ms", (1, NH), F32, "p4", ("p6",))
    srs_d = scr("s_rs", (NH, T), F32, "p6", ("p6",))

    with ExitStack() as ctx:
        def sb(name, shape, dt=F32):
            return ctx.enter_context(nc.sbuf_tensor(name, list(shape), dt))

        def pst(name, shape, dt=F32):
            return ctx.enter_context(nc.psum_tensor(name, list(shape), dt))

        p = Prog(nc, ctx)

        cst = sb("cst", (128, 4))
        nc.const_aps.register(F32, 1e-5, cst[:, 0:1])
        nc.const_aps.register(F32, 1.000001, cst[:, 1:2])
        p.block()
        p.add("vector", lambda v: v.memset(cst[:, 0:1], 1e-5), writes=[cst[:, 0:1]])
        p.add("vector", lambda v: v.memset(cst[:, 1:2], 1.000001), writes=[cst[:, 1:2]])


        def dma(kind, dst, src, transpose=False):
            if transpose:
                fn = lambda e, d=dst, s=src: e.dma_start_transpose(d, s)
            else:
                fn = lambda e, d=dst, s=src: e.dma_start(d, s)
            return p.add(kind, fn, reads=[src], writes=[dst])

        def mmgrp(ps_, pairs, tile=None):
            def fn(t, ps_=ps_, pairs=pairs, tile=tile):
                n = len(pairs)
                for i, (l, r) in enumerate(pairs):
                    ins = t.matmul(ps_, l, r, start=(i == 0), stop=(i == n - 1),
                                   tile_position=tile)
                return ins
            rd = [a for pr in pairs for a in pr]
            return p.add("tensor", fn, reads=rd, writes=[ps_])

        if "p0" in phases or "p1" in phases:
            with ExitStack() as pc:
                psb = lambda n, s, d=F32: pc.enter_context(nc.sbuf_tensor(n, list(s), d))
                xt = [psb(f"p0x{i}", (128, D)) for i in range(8)]
                lnt = [psb(f"p0l{i}", (128, D), F16) for i in range(4)]
                bn4 = psb("p0b4", (128, 48))
                mv4 = psb("p0m4", (128, 8))
                sd4 = psb("p0d4", (128, 8))
                ri4 = psb("p0r4", (128, 8))
                win = psb("p1w", (128, NCC * 2048), F16)
                xlnf = psb("p1xf", (128, NCC * T), F16)
                gst = [psb(f"p1g{i}", (128, 8 * 512), F16) for i in range(2)]
                ast = [psb(f"p1a{i}", (128, 8 * 512), F16) for i in range(2)]
                pp = [pc.enter_context(nc.psum_tensor(f"p1p{i}", [128, 512], F32))
                      for i in range(2)]
                p.block()

                def p0_load(i):
                    dma("gpsimd_d", xt[i % 8][:, :], x_d.ap()[i * 128:(i + 1) * 128, :])

                def p0_qt(qt):
                    for k in range(4):
                        i = qt * 4 + k
                        b = i % 8
                        p.add("vector",
                              (lambda v, b=b, k=k: v.bn_stats(
                                  bn4[:, 12 * k:12 * k + 6], xt[b][:, 0:512])),
                              reads=[xt[b][:, 0:512]], writes=[bn4[:, 12 * k:12 * k + 6]])
                        p.add("vector",
                              (lambda v, b=b, k=k: v.bn_stats(
                                  bn4[:, 12 * k + 6:12 * k + 12], xt[b][:, 512:1024])),
                              reads=[xt[b][:, 512:1024]],
                              writes=[bn4[:, 12 * k + 6:12 * k + 12]])
                        p.add("vector",
                              (lambda v, k=k: v.bn_aggr(
                                  mv4[:, 2 * k:2 * k + 2], bn4[:, 12 * k:12 * k + 12])),
                              reads=[bn4[:, 12 * k:12 * k + 12]],
                              writes=[mv4[:, 2 * k:2 * k + 2]])
                    p.add("scalar",
                          lambda s: s.activation(
                              sd4[:, 0:4], bass.AP(mv4, 1, [[8, 128], [2, 4]]),
                              Act.Sqrt, bias=1e-5),
                          reads=[mv4[:, :], cst[:, 0:1]], writes=[sd4[:, 0:4]])
                    p.add("vector", lambda v: v.reciprocal(ri4[:, 0:4], sd4[:, 0:4]),
                          reads=[sd4[:, 0:4]], writes=[ri4[:, 0:4]])
                    for k in range(4):
                        i = qt * 4 + k
                        b = i % 8
                        p.add("vector",
                              (lambda v, b=b, k=k: v.tensor_scalar(
                                  lnt[k][:, :], xt[b][:, :], mv4[:, 2 * k:2 * k + 1],
                                  ri4[:, k:k + 1],
                                  Alu.subtract, Alu.mult)),
                              reads=[xt[b][:, :], mv4[:, 2 * k:2 * k + 1],
                                     ri4[:, k:k + 1]],
                              writes=[lnt[k][:, :]])
                        dma("gpsimd_d", sxln1_d.ap()[i * 128:(i + 1) * 128, :],
                            lnt[k][:, :])

                def p1_trans(c2):
                    r0 = c2 * 1024
                    for cc in range(NCC):
                        dma("sync", xlnf[:, cc * T + r0:cc * T + r0 + 1024],
                            sxln1_d.ap()[r0:r0 + 1024,
                                         cc * 128:(cc + 1) * 128], transpose=True)

                def p1_proj(tc):
                    b = tc % 2
                    for fc in range(16):
                        idx = tc * 16 + fc
                        ps_ = pp[idx % 2][:, :]
                        mmgrp(ps_, [
                            (win[:, cc * 2048 + fc * 128:cc * 2048 + (fc + 1) * 128],
                             xlnf[:, cc * T + tc * 512:cc * T + (tc + 1) * 512])
                            for cc in range(NCC)])
                        if fc < 8:
                            dst = gst[b][:, fc * 512:(fc + 1) * 512]
                            p.add("scalar",
                                  (lambda s, d=dst, ps2=ps_: s.activation(d, ps2, Act.Gelu)),
                                  reads=[ps_], writes=[dst])
                        else:
                            dst = ast[b][:, (fc - 8) * 512:(fc - 7) * 512]
                            p.add("scalar",
                                  (lambda s, d=dst, ps2=ps_: s.copy(d, ps2)),
                                  reads=[ps_], writes=[dst])
                    for fc8 in range(8):
                        dma("gpsimd_d", sgelu_d.ap()[fc8, :, tc * 512:(tc + 1) * 512],
                            gst[b][:, fc8 * 512:(fc8 + 1) * 512])
                    for fc8 in range(8):
                        dma("gpsimd_d", sxa_d.ap()[fc8, :, tc * 512:(tc + 1) * 512],
                            ast[b][:, fc8 * 512:(fc8 + 1) * 512])

                for cc in range(NCC):
                    dma("gpsimd_d", win[:, cc * 2048:(cc + 1) * 2048],
                        winT_d.ap()[cc * 128:(cc + 1) * 128, :])
                for k in range(8):
                    p0_load(k)
                for qt in range(10):
                    if qt < 8:
                        p0_qt(qt)
                        if qt + 2 < 8:
                            for k in range(4):
                                p0_load((qt + 2) * 4 + k)
                    if qt % 2 == 1 and (qt - 1) // 2 < 4:
                        p1_trans((qt - 1) // 2)
                    if 2 <= qt <= 9:
                        p1_proj(qt - 2)

        if "p2" in phases:
            with ExitStack() as pc:
                psb = lambda n, s, d=F32: pc.enter_context(nc.sbuf_tensor(n, list(s), d))
                wg = psb("p2wg", (128, NCC * 2048), F16)
                wo = psb("p2wo", (128, NCC * 1024), F16)
                wcv = psb("p2wc", (128, NCC * 4))
                bcv = psb("p2bc", (128, NCC))
                bga = psb("p2bg", (128, 16))
                fbt = psb("p2fb", (128, NCC))
                e1 = psb("p2e1", (128, NCC))
                spt = psb("p2sp", (128, NCC))
                aco = psb("p2ac", (128, NCC))
                W = NCC * P2TC
                xa = [psb(f"p2xa{i}", (128, NCC * 260), F16) for i in range(2)]
                gel = [psb(f"p2ge{i}", (128, W), F16) for i in range(2)]
                xcf = [psb(f"p2xc{i}", (128, W), F16) for i in range(2)]
                idn16 = psb("p2id", (128, 128), F16)
                dgt = psb("p2dg", (128, 32 * 128), F16)
                sf = [psb("p2sf0", (128, W))]
                si = [psb(f"p2si{i}", (128, W), F16) for i in range(2)]
                al = [psb(f"p2al{i}", (128, W)) for i in range(2)]
                bt = [psb(f"p2bt{i}", (128, W), F16) for i in range(2)]
                ub = psb("p2ub", (128, W), F16)
                hbuf = [psb(f"p2h{i}", (128, W)) for i in range(2)]
                yt = [psb(f"p2y{i}", (128, W), F16) for i in range(2)]
                xt2 = [psb(f"p2x{i}", (128, 2048)) for i in range(3)]
                x1t = [psb(f"p2o{i}", (128, 2048)) for i in range(2)]
                ln2t = [psb(f"p2n{i}", (128, 2048), F16) for i in range(2)]
                bn2 = psb("p2b2", (128, 24))
                mv2 = psb("p2m2", (128, 4))
                sd2 = psb("p2d2", (128, 2))
                ri2 = psb("p2r2", (128, 2))
                pg = [pc.enter_context(nc.psum_tensor(f"p2pg{i}", [128, 512], F32))
                      for i in range(2)]
                pcv = [pc.enter_context(nc.psum_tensor(f"p2pc{i}", [128, 512], F32))
                       for i in range(2)]
                po = [pc.enter_context(nc.psum_tensor(f"p2po{i}", [128, 512], F32))
                      for i in range(4)]
                p.block()
                for cc in range(NCC):
                    dma("gpsimd_d", wg[:, cc * 2048:(cc + 1) * 2048],
                        wgT_d.ap()[cc * 128:(cc + 1) * 128, :])
                for cc in range(NCC):
                    dma("gpsimd_d", wo[:, cc * 1024:(cc + 1) * 1024],
                        woT_d.ap()[cc * 128:(cc + 1) * 128, :])
                dma("gpsimd_d", wcv[:, :].rearrange("p (c k) -> p c k", c=NCC),
                    wcv_d.ap())
                dma("gpsimd_d", bcv[:, :], bcv_d.ap())
                dma("gpsimd_d", bga[:, :], bga_d.ap())
                dma("gpsimd_d", fbt[:, :], fb_d.ap())
                dma("gpsimd_d", idn16[:, :], ident_d.ap())
                p.add("scalar", lambda s: s.activation(e1[:, :], fbt[:, :], Act.Exp),
                      reads=[fbt[:, :]], writes=[e1[:, :]])
                p.add("scalar", lambda s: s.activation(spt[:, :], e1[:, :], Act.Ln, bias=1.0),
                      reads=[e1[:, :]], writes=[spt[:, :]])
                p.add("vector", lambda v: v.tensor_scalar(aco[:, :], spt[:, :], -8.0,
                                                          None, Alu.mult),
                      reads=[spt[:, :]], writes=[aco[:, :]])
                for cc in range(NCC):
                    p.add("vector",
                          (lambda v, cc=cc: v.memset(xa[0][:, cc * 260:cc * 260 + 3], 0.0)),
                          writes=[xa[0][:, cc * 260:cc * 260 + 3]])
                for cc in range(NCC):
                    for tap in range(4):
                        j = cc * 4 + tap
                        dst = dgt[:, j * 128:(j + 1) * 128]
                        p.add("vector",
                              (lambda v, d=dst, j=j: v.tensor_scalar(
                                  d, idn16[:, :], wcv[:, j:j + 1], None, Alu.mult)),
                              reads=[idn16[:, :], wcv[:, j:j + 1]], writes=[dst])

                def p2_loads_xa(tc):
                    b = tc % 2
                    for cc in range(NCC):
                        if tc == 0:
                            dma("sync", xa[b][:, cc * 260 + 3:cc * 260 + 259],
                                sxa_d.ap()[cc, :, 0:256])
                        else:
                            dma("sync", xa[b][:, cc * 260:cc * 260 + 259],
                                sxa_d.ap()[cc, :, tc * 256 - 3:tc * 256 + 256])
                    for tbi in range(2):
                        r0 = tc * 256 + tbi * 128
                        dma("sync", xt2[tc % 3][:, tbi * 1024:(tbi + 1) * 1024],
                            x_d.ap()[r0:r0 + 128, :])

                def p2_loads_gel(tc):
                    b = tc % 2
                    for cc in range(NCC):
                        dma("sync", gel[b][:, cc * 256:(cc + 1) * 256],
                            sgelu_d.ap()[cc, :, tc * 256:(tc + 1) * 256])

                def p2_conv(tc):
                    b = tc % 2
                    for cc in range(NCC):
                        xo = cc * 260
                        sl = slice(cc * 256, (cc + 1) * 256)
                        ps_ = pcv[cc % 2][:, 0:256]
                        mmgrp(ps_, [
                            (dgt[:, (cc * 4 + tap) * 128:(cc * 4 + tap + 1) * 128],
                             xa[b][:, xo + tap:xo + tap + 256])
                            for tap in range(4)])
                        p.add("scalar",
                              (lambda s, d=xcf[b][:, sl], ps2=ps_, cc=cc: s.activation(
                                  d, ps2, Act.Identity, bias=bcv[:, cc:cc + 1])),
                              reads=[ps_, bcv[:, cc:cc + 1]], writes=[xcf[b][:, sl]])

                def p2_gates(tc):
                    b = tc % 2
                    for fc in range(16):
                        idx = tc * 16 + fc
                        ps_ = pg[idx % 2][:, 0:256]
                        mmgrp(ps_, [
                            (wg[:, cc * 2048 + fc * 128:cc * 2048 + (fc + 1) * 128],
                             xcf[b][:, cc * 256:(cc + 1) * 256])
                            for cc in range(NCC)])
                        if fc < 8:
                            dst = sf[0][:, fc * 256:(fc + 1) * 256]
                        else:
                            dst = si[b][:, (fc - 8) * 256:(fc - 7) * 256]
                        p.add("scalar",
                              (lambda s, d=dst, ps2=ps_, fc=fc: s.activation(
                                  d, ps2, Act.Sigmoid, bias=bga[:, fc:fc + 1])),
                              reads=[ps_, bga[:, fc:fc + 1]], writes=[dst])

                def p2_alphabeta(tc):
                    b = tc % 2
                    for cc in range(NCC):
                        sl = slice(cc * 256, (cc + 1) * 256)
                        p.add("scalar",
                              (lambda s, sl=sl, cc=cc, b=b: s.activation(
                                  al[b][:, sl], sf[0][:, sl], Act.Exp,
                                  scale=aco[:, cc:cc + 1])),
                              reads=[sf[0][:, sl], aco[:, cc:cc + 1]],
                              writes=[al[b][:, sl]])
                    p.add("vector", (lambda v, b=b: v.tensor_tensor(
                        sf[0][:, :], al[b][:, :], al[b][:, :], Alu.mult)),
                          reads=[al[b][:, :]], writes=[sf[0][:, :]])
                    p.add("scalar",
                          (lambda s, b=b: s.activation(bt[b][:, :], sf[0][:, :], Act.Sqrt,
                                                       bias=1.000001, scale=-1.0)),
                          reads=[sf[0][:, :], cst[:, 1:2]], writes=[bt[b][:, :]])

                def p2_uscany(tc):
                    b = tc % 2
                    p.add("vector", (lambda v, b=b: v.tensor_tensor(
                        ub[:, :], bt[b][:, :], si[b][:, :], Alu.mult)),
                          reads=[bt[b][:, :], si[b][:, :]], writes=[ub[:, :]])
                    p.add("vector", (lambda v, b=b: v.tensor_tensor(
                        si[b][:, :], ub[:, :], xcf[b][:, :], Alu.mult)),
                          reads=[ub[:, :], xcf[b][:, :]], writes=[si[b][:, :]])
                    for cc in range(NCC):
                        sl = slice(cc * 256, (cc + 1) * 256)
                        if tc == 0:
                            ini = 0.0
                            rd = [al[b][:, sl], si[b][:, sl]]
                        else:
                            ini = hbuf[1 - b][:, cc * 256 + 255:cc * 256 + 256]
                            rd = [al[b][:, sl], si[b][:, sl], ini]
                        p.add("vector",
                              (lambda v, sl=sl, b=b, ini=ini: v.tensor_tensor_scan(
                                  hbuf[b][:, sl], al[b][:, sl], si[b][:, sl], ini,
                                  Alu.mult, Alu.add)),
                              reads=rd, writes=[hbuf[b][:, sl]])
                    p.add("vector", (lambda v, b=b: v.tensor_tensor(
                        yt[b][:, :], gel[b][:, :], hbuf[b][:, :], Alu.mult)),
                          reads=[gel[b][:, :], hbuf[b][:, :]], writes=[yt[b][:, :]])

                def p2_hawk(tc):
                    b = tc % 2
                    for tbi in range(2):
                        for ch in range(2):
                            g = 4 * tc + 2 * tbi + ch
                            ps_ = po[g % 4][:, :]
                            mmgrp(ps_, [
                                (yt[b][:, cc * 256 + tbi * 128:cc * 256 + (tbi + 1) * 128],
                                 wo[:, cc * 1024 + ch * 512:cc * 1024 + (ch + 1) * 512])
                                for cc in range(NCC)])

                def p2_resid(tc):
                    b = tc % 2
                    for tbi in range(2):
                        for ch in range(2):
                            g = 4 * tc + 2 * tbi + ch
                            ps_ = po[g % 4][:, :]
                            sl = slice(tbi * 1024 + ch * 512, tbi * 1024 + (ch + 1) * 512)
                            p.add("vector",
                                  (lambda v, sl=sl, b=b, ps2=ps_: v.tensor_tensor(
                                      x1t[b][:, sl], ps2, xt2[tc % 3][:, sl], Alu.add)),
                                  reads=[ps_, xt2[tc % 3][:, sl]], writes=[x1t[b][:, sl]])
                        t0 = tbi * 1024
                        p.add("vector", (lambda v, b=b, tbi=tbi, t0=t0: v.bn_stats(
                            bn2[:, 12 * tbi:12 * tbi + 6], x1t[b][:, t0:t0 + 512])),
                              reads=[x1t[b][:, t0:t0 + 512]],
                              writes=[bn2[:, 12 * tbi:12 * tbi + 6]])
                        p.add("vector", (lambda v, b=b, tbi=tbi, t0=t0: v.bn_stats(
                            bn2[:, 12 * tbi + 6:12 * tbi + 12],
                            x1t[b][:, t0 + 512:t0 + 1024])),
                              reads=[x1t[b][:, t0 + 512:t0 + 1024]],
                              writes=[bn2[:, 12 * tbi + 6:12 * tbi + 12]])
                        p.add("vector", (lambda v, tbi=tbi: v.bn_aggr(
                            mv2[:, 2 * tbi:2 * tbi + 2], bn2[:, 12 * tbi:12 * tbi + 12])),
                              reads=[bn2[:, 12 * tbi:12 * tbi + 12]],
                              writes=[mv2[:, 2 * tbi:2 * tbi + 2]])
                        p.add("scalar", (lambda s, tbi=tbi: s.activation(
                            sd2[:, tbi:tbi + 1], mv2[:, 2 * tbi + 1:2 * tbi + 2],
                            Act.Sqrt, bias=1e-5)),
                              reads=[mv2[:, 2 * tbi + 1:2 * tbi + 2], cst[:, 0:1]],
                              writes=[sd2[:, tbi:tbi + 1]])
                        p.add("vector", (lambda v, tbi=tbi: v.reciprocal(
                            ri2[:, tbi:tbi + 1], sd2[:, tbi:tbi + 1])),
                              reads=[sd2[:, tbi:tbi + 1]], writes=[ri2[:, tbi:tbi + 1]])
                        p.add("vector", (lambda v, b=b, tbi=tbi, t0=t0: v.tensor_scalar(
                            ln2t[b][:, t0:t0 + 1024], x1t[b][:, t0:t0 + 1024],
                            mv2[:, 2 * tbi:2 * tbi + 1], ri2[:, tbi:tbi + 1],
                            Alu.subtract, Alu.mult)),
                              reads=[x1t[b][:, t0:t0 + 1024], mv2[:, 2 * tbi:2 * tbi + 1],
                                     ri2[:, tbi:tbi + 1]],
                              writes=[ln2t[b][:, t0:t0 + 1024]])
                    for tbi in range(2):
                        r0 = tc * 256 + tbi * 128
                        dma("sync", sx1_d.ap()[r0:r0 + 128, :],
                            x1t[b][:, tbi * 1024:(tbi + 1) * 1024])
                        dma("sync", sxln2_d.ap()[r0:r0 + 128, :],
                            ln2t[b][:, tbi * 1024:(tbi + 1) * 1024])

                p2_loads_xa(0)
                p2_loads_gel(0)
                for tc in range(NP2):
                    p2_conv(tc)
                    if tc + 1 < NP2:
                        p2_loads_xa(tc + 1)
                    p2_gates(tc)
                    if tc >= 1:
                        p2_uscany(tc - 1)
                        if tc + 1 < NP2:
                            p2_loads_gel(tc + 1)
                        p2_hawk(tc - 1)
                    elif tc + 1 < NP2:
                        p2_loads_gel(tc + 1)
                    p2_alphabeta(tc)
                    if tc >= 1:
                        p2_resid(tc - 1)
                p2_uscany(NP2 - 1)
                p2_hawk(NP2 - 1)
                p2_resid(NP2 - 1)

        if ("p4" in phases) or ("p6" in phases):
            kfm = sb("kfm", (128, 4 * T), F16)
            vals = sb("vals", (128, NTB * NH * DH), BF16)
            mbias = sb("mbias", (128, NH))
        if "p4" in phases:
            with ExitStack() as pc:
                psb = lambda n, s, d=F32: pc.enter_context(nc.sbuf_tensor(n, list(s), d))
                xlnf2 = psb("p4xf", (128, NCC * T), F16)
                gm = psb("p4gm", (128, NTB * 64))
                pca = ExitStack()
                asb = lambda n, s, d=F32: pca.enter_context(nc.sbuf_tensor(n, list(s), d))
                wqs = asb("p4wq", (128, NCC * 512), F16)
                wks = asb("p4wk", (128, NCC * 512), F16)
                wgs = asb("p4wg", (128, NCC * 64), F16)
                ones = asb("p4on", (128, 32), F16)
                identf = asb("p4id", (128, 128))
                qst = [asb(f"p4q{i}", (128, 512), F16) for i in range(2)]
                sqt = [asb(f"p4s{i}", (128, 512), F16) for i in range(2)]
                slotb = asb("p4sl", (1, 128))
                hmax = asb("p4hm", (1, 16))
                G4 = asb("p4G", (128, 256))
                L4 = asb("p4L", (128, 256))
                aA1 = asb("p4a1", (128, 128))
                aA2 = asb("p4a2", (128, 64))
                aM8 = asb("p4a8", (128, 32))
                aF = asb("p4aF", (128, 256))
                aP = asb("p4aP", (128, 256))
                aPv = asb("p4aV", (128, 256))
                aD = asb("p4aD", (128, 256))
                aS = asb("p4aS", (128, 256))
                aone = asb("p4a1s", (128, 256))
                gmf = asb("p4gf", (64, T))
                mprod = asb("p4mp", (1, 8))
                mrt = asb("p4mr", (1, 8))
                mneg = asb("p4mn", (1, 8))
                pq = [pca.enter_context(nc.psum_tensor(f"p4pq{i}", [128, 512], F32))
                      for i in range(2)]
                pn = [pca.enter_context(nc.psum_tensor(f"p4pn{i}", [1, 512], F32))
                      for i in range(2)]
                pg4 = [pca.enter_context(nc.psum_tensor(f"p4pg{i}", [128, 512], F32))
                       for i in range(2)]
                pgt = pca.enter_context(nc.psum_tensor("p4pt", [64, 128], F32))
                p.block()
                for cc in range(NCC):
                    dma("sync", xlnf2[:, cc * T:(cc + 1) * T],
                        sxln2_d.ap()[:, cc * 128:(cc + 1) * 128], transpose=True)
                for cc in range(NCC):
                    dma("gpsimd_d", wqs[:, cc * 512:(cc + 1) * 512],
                        wq_d.ap()[cc * 128:(cc + 1) * 128, :])
                for cc in range(NCC):
                    dma("gpsimd_d", wks[:, cc * 512:(cc + 1) * 512],
                        wk_d.ap()[cc * 128:(cc + 1) * 128, :])
                for cc in range(NCC):
                    dma("gpsimd_d", wgs[:, cc * 64:(cc + 1) * 64],
                        wgate_d.ap()[cc * 128:(cc + 1) * 128, :])
                dma("gpsimd_d", ones[:, :], ones_d.ap())
                dma("gpsimd_d", identf[:, :], ident_d.ap())
                p.add("vector", lambda v: v.memset(aone[:, :], 1.0), writes=[aone[:, :]])
                p.add("vector", lambda v: v.memset(aPv[:, 0:8], 0.0), writes=[aPv[:, 0:8]])

                def p4_norm_mms(j):
                    for hhh in range(2):
                        nidx = 2 * j + hhh
                        ps_ = pn[nidx % 2][:, :]
                        p.add("tensor",
                              (lambda t, hhh=hhh, j=j, ps2=ps_: t.matmul(
                                  ps2, ones[hhh * 64:(hhh + 1) * 64, 0:1],
                                  sqt[j % 2][hhh * 64:(hhh + 1) * 64, :],
                                  start=True, stop=True,
                                  tile_position=(hhh * 64, 0))),
                              reads=[ones[hhh * 64:(hhh + 1) * 64, 0:1],
                                     sqt[j % 2][hhh * 64:(hhh + 1) * 64, :]],
                              writes=[ps_])
                        dst = slotb[0:1, nidx:nidx + 1]
                        p.add("vector",
                              (lambda v, d=dst, ps2=ps_: v.tensor_reduce(
                                  d, ps2, AX.X, Alu.max)),
                              reads=[ps_], writes=[dst])

                for idx in range(64):
                    w = wqs if idx < 32 else wks
                    hp = (idx % 32) // 8
                    tg = idx % 8
                    ps_ = pq[idx % 2][:, :]
                    mmgrp(ps_, [
                        (w[:, cc * 512 + hp * 128:cc * 512 + (hp + 1) * 128],
                         xlnf2[:, cc * T + tg * 512:cc * T + (tg + 1) * 512])
                        for cc in range(NCC)])
                    if idx < 32:
                        dst = qst[idx % 2][:, :]
                        p.add("scalar",
                              (lambda s, d=dst, ps2=ps_: s.activation(
                                  d, ps2, Act.Copy, scale=0.125)),
                              reads=[ps_], writes=[dst])
                        dma("sync", sq_d.ap()[hp, :, tg * 512:(tg + 1) * 512], dst)
                    else:
                        dst = kfm[:, hp * T + tg * 512:hp * T + (tg + 1) * 512]
                        p.add("scalar",
                              (lambda s, d=dst, ps2=ps_: s.copy(d, ps2)),
                              reads=[ps_], writes=[dst])
                    sdst = sqt[idx % 2][:, :]
                    p.add("scalar",
                          (lambda s, d=sdst, ps2=ps_: s.activation(d, ps2, Act.Square)),
                          reads=[ps_], writes=[sdst])
                    if idx >= 1:
                        p4_norm_mms(idx - 1)
                p4_norm_mms(63)
                for side in range(2):
                    for hp in range(4):
                        for hhh in range(2):
                            h = hp * 2 + hhh
                            base = side * 64 + hp * 16 + hhh
                            src = bass.AP(slotb.tensor if hasattr(slotb, "tensor")
                                          else None, 0, []) if False else None
                            sl = slotb[0:1, base:base + 15:2]
                            dst = hmax[0:1, side * 8 + h:side * 8 + h + 1]
                            p.add("vector",
                                  (lambda v, d=dst, s_=sl: v.tensor_reduce(
                                      d, s_, AX.X, Alu.max)),
                                  reads=[sl], writes=[dst])
                p.add("vector", lambda v: v.tensor_tensor(
                    mprod[:, :], hmax[0:1, 0:8], hmax[0:1, 8:16], Alu.mult),
                      reads=[hmax[:, :]], writes=[mprod[:, :]])
                p.add("scalar", lambda s: s.activation(mrt[:, :], mprod[:, :], Act.Sqrt),
                      reads=[mprod[:, :]], writes=[mrt[:, :]])
                p.add("vector", lambda v: v.tensor_scalar(
                    mneg[:, :], mrt[:, :], -0.125, None, Alu.mult),
                      reads=[mrt[:, :]], writes=[mneg[:, :]])
                dma("sync", sms_d.ap()[0:1, :], mneg[0:1, :])
                dma("sync", mbias[:, :], bass.AP(sms_d, 0, [[0, 128], [1, NH]]))

                for tb in range(NTB):
                    ps_ = pg4[tb % 2][:, 0:64]
                    mmgrp(ps_, [
                        (xlnf2[:, cc * T + tb * 128:cc * T + (tb + 1) * 128],
                         wgs[:, cc * 64:(cc + 1) * 64])
                        for cc in range(NCC)])
                    dst = G4[:, (tb % 4) * 64:(tb % 4 + 1) * 64]
                    p.add("scalar",
                          (lambda s, d=dst, ps2=ps_: s.activation(d, ps2, Act.Sigmoid)),
                          reads=[ps_], writes=[dst])
                    ldst = L4[:, (tb % 4) * 64:(tb % 4 + 1) * 64]
                    p.add("scalar",
                          (lambda s, d=ldst, ps2=ps_: s.copy(d, ps2)),
                          reads=[ps_], writes=[ldst])
                    if tb % 4 == 3:
                        tb0 = tb - 3
                        p.add("vector", lambda v: v.tensor_tensor(
                            aA1[:, :], L4[:, 0:256:2], L4[:, 1:256:2], Alu.max),
                              reads=[L4[:, :]], writes=[aA1[:, :]])
                        p.add("vector", lambda v: v.tensor_tensor(
                            aA2[:, :], aA1[:, 0:128:2], aA1[:, 1:128:2], Alu.max),
                              reads=[aA1[:, :]], writes=[aA2[:, :]])
                        p.add("vector", lambda v: v.tensor_tensor(
                            aM8[:, :], aA2[:, 0:64:2], aA2[:, 1:64:2], Alu.max),
                              reads=[aA2[:, :]], writes=[aM8[:, :]])
                        mb_ = aM8[:, :].unsqueeze(2).broadcast_to([128, 32, 8])
                        p.add("vector", (lambda v, mb_=mb_: v.tensor_tensor(
                            aF[:, :].rearrange("p (a c) -> p a c", a=32),
                            L4[:, :].rearrange("p (a c) -> p a c", a=32),
                            mb_, Alu.is_ge)),
                              reads=[L4[:, :], aM8[:, :]], writes=[aF[:, :]])
                        p.add("vector", lambda v: v.tensor_tensor_scan(
                            aP[:, :], aone[:, :], aF[:, :], 0.0, Alu.mult, Alu.add),
                              reads=[aone[:, :], aF[:, :]], writes=[aP[:, :]])
                        pb_ = aP[:, 7:255:8].unsqueeze(2).broadcast_to([128, 31, 8])
                        p.add("vector", (lambda v, pb_=pb_: v.tensor_copy(
                            aPv[:, 8:256].rearrange("p (a c) -> p a c", a=31), pb_)),
                              reads=[aP[:, :]], writes=[aPv[:, 8:256]])
                        p.add("vector", lambda v: v.tensor_tensor(
                            aD[:, :], aP[:, :], aPv[:, :], Alu.subtract),
                              reads=[aP[:, :], aPv[:, :]], writes=[aD[:, :]])
                        p.add("vector", lambda v: v.tensor_scalar(
                            aS[:, :], aD[:, :], 1.0, None, Alu.is_equal),
                              reads=[aD[:, :]], writes=[aS[:, :]])
                        p.add("vector", lambda v: v.tensor_tensor(
                            aD[:, :], aS[:, :], aF[:, :], Alu.mult),
                              reads=[aS[:, :], aF[:, :]], writes=[aD[:, :]])
                        p.add("vector", (lambda v, tb0=tb0: v.tensor_tensor(
                            gm[:, tb0 * 64:(tb0 + 4) * 64], aD[:, :],
                            G4[:, :], Alu.mult)),
                              reads=[aD[:, :], G4[:, :]],
                              writes=[gm[:, tb0 * 64:(tb0 + 4) * 64]])
                for tb in range(NTB):
                    p.add("tensor", (lambda t, tb=tb: t.transpose(
                        pgt[:, :], gm[:, tb * 64:(tb + 1) * 64], identf[:, :])),
                          reads=[gm[:, tb * 64:(tb + 1) * 64], identf[:, :]],
                          writes=[pgt[:, :]])
                    dst = gmf[:, tb * 128:(tb + 1) * 128]
                    p.add("scalar", (lambda s, d=dst: s.copy(d, pgt[:, :])),
                          reads=[pgt[:, :]], writes=[dst])
                dma("sync", sgmf_d.ap()[:, :], gmf[:, :])

                pca.close()
                p.block()
                wvs = [psb(f"p4wv{i}", (128, NCC * 512), F16) for i in range(2)]
                vQ = psb("p4vQ", (128, 2048))
                pvb2 = [pc.enter_context(nc.psum_tensor(f"p4pv{i}", [128, 2048], F32))
                        for i in range(2)]
                for h in range(NH):
                    for cc in range(NCC):
                        src = bass.AP(wv_d, cc * 128 * (NH * NE * DH) + h * (NE * DH),
                                      [[NH * NE * DH, 128], [1, NE * DH]])
                        dma("gpsimd_d", wvs[h % 2][:, cc * 512:(cc + 1) * 512], src)
                    for tb in range(NTB):
                        vidx = h * NTB + tb
                        pvb = pvb2[(vidx // 4) % 2]
                        ps_ = pvb[:, (tb % 4) * 512:(tb % 4 + 1) * 512]
                        mmgrp(ps_, [
                            (xlnf2[:, cc * T + tb * 128:cc * T + (tb + 1) * 128],
                             wvs[h % 2][:, cc * 512:(cc + 1) * 512])
                            for cc in range(NCC)])
                        if tb % 4 != 3:
                            continue
                        tb0 = tb - 3
                        bb = bass.AP(gm, tb0 * 64 + h * 8,
                                     [[NTB * 64, 128], [64, 4], [1, 8], [0, 64]])
                        psall = pvb[:, :]
                        p.add("vector", (lambda v, bb=bb, pvb=pvb: v.tensor_tensor(
                            vQ[:, :].rearrange("p (t e d) -> p t e d", t=4, e=8),
                            pvb[:, :].rearrange("p (t e d) -> p t e d", t=4, e=8),
                            bb, Alu.mult)),
                              reads=[psall,
                                     gm[:, tb0 * 64:(tb0 + 4) * 64]],
                              writes=[vQ[:, :]])
                        vdst = bass.AP(vals, (tb0 * NH + h) * 64,
                                       [[NTB * NH * 64, 128], [NH * 64, 4], [1, 64]])
                        vsrc = vQ[:, :].rearrange("p (t e d) -> p t d e", t=4, e=8)

                        def _vred(v, d=vdst, s_=vsrc):
                            with nc.allow_low_precision("fp32-internal reduce, bf16 out"):
                                return v.tensor_reduce(d, s_, AX.X, Alu.add)
                        p.add("vector", _vred, reads=[vQ[:, :]],
                              writes=[vals[:, tb0 * NH * 64:(tb0 + 4) * NH * 64]])

        if "p6" in phases:
            with ExitStack() as pc:
                psb = lambda n, s, d=F32: pc.enter_context(nc.sbuf_tensor(n, list(s), d))
                woes = psb("p6wo", (128, NE * 4 * 1024), F16)
                tri = psb("p6tr", (128, 4 * 512), BF16)
                ones32b = psb("p6on1", (128, 32), BF16)
                qg_t = [psb(f"p6q{i}", (128, 4 * 512), F16) for i in range(2)]
                at = [psb(f"p6a{i}", (128, 512), BF16) for i in range(16)]
                rs8a_t = psb("p6ra", (128, 1024))
                rs8a = [rs8a_t[:, 0:512], rs8a_t[:, 512:1024]]
                rsx = [psb(f"p6rx{i}", (128, 512)) for i in range(4)]
                rsz = psb("p6rz", (128, 1024))
                onorm = psb("p6on", (128, 4 * 512), F16)
                gmx = [psb("p6gx0", (128, 4 * 2048), F16)]
                msk = [psb(f"p6m{i}", (128, 512), F16) for i in range(2)]
                x1b = [psb(f"p6x{i}", (128, 1024)) for i in range(2)]
                ps8 = [pc.enter_context(nc.psum_tensor(f"p6p{i}", [128, 512], F32))
                       for i in range(8)]
                p.block()
                for e in range(NE):
                    for hp in range(4):
                        dma("gpsimd_d",
                            woes[:, (e * 4 + hp) * 1024:(e * 4 + hp + 1) * 1024],
                            woe_d.ap()[e, hp, :, :])
                dma("gpsimd_d", tri[:, :].rearrange("p (a c) -> p a c", a=4),
                    bass.AP(tri_d, 0, [[512, 128], [128 * 512, 4], [1, 512]]))
                dma("gpsimd_d", ones32b[:, :], ones_d.ap())

                for hp in range(4):
                    dma("sync", qg_t[0][:, hp * 512:(hp + 1) * 512],
                        sq_d.ap()[hp, :, 0:512])
                def gmx_load(eh, g):
                    gb = gmx[0]
                    for hp in range(4):
                        for hh in range(2):
                            h = 2 * hp + hh
                            src = bass.AP(
                                sgmf_d, (h * 8 + eh * 4) * T + g * 512,
                                [[0, 64], [T, 4], [1, 512]])
                            dma("gpsimd_d",
                                gb[hh * 64:(hh + 1) * 64,
                                   hp * 2048:(hp + 1) * 2048]
                                .rearrange("p (a c) -> p a c", a=4), src)

                eorder = [1, 2, 0, 3]

                def p6_chain_head(g):
                    p.add("scalar", lambda s: s.activation(
                        rsz[:, :], rs8a_t[:, :], Act.Ln),
                          reads=[rs8a_t[:, :]], writes=[rsz[:, :]])
                    p.add("scalar", lambda s: s.activation(
                        rs8a_t[:, :], rsz[:, :], Act.Exp, scale=-1.0),
                          reads=[rsz[:, :]], writes=[rs8a_t[:, :]])
                    for qd in range(2):
                        dma("sync", srs_d.ap()[qd * 4:(qd + 1) * 4,
                                               g * 512:(g + 1) * 512],
                            bass.AP(rs8a_t, qd * 512, [[1024 * 32, 4], [1, 512]]))
                    for hp in range(4):
                        for hh in range(2):
                            h = 2 * hp + hh
                            src = bass.AP(srs_d, h * T + g * 512,
                                          [[0, 64], [1, 512]])
                            dma("sync", rsx[hp][hh * 64:(hh + 1) * 64, :], src)
                    for hp in range(4):
                        p.add("vector", (lambda v, hp=hp: v.tensor_tensor(
                            onorm[:, hp * 512:(hp + 1) * 512], ps8[2 + hp][:, :],
                            rsx[hp][:, :], Alu.mult)),
                              reads=[ps8[2 + hp][:, :], rsx[hp][:, :]],
                              writes=[onorm[:, hp * 512:(hp + 1) * 512]])
                    if g + 1 < 8:
                        for qd in range(2):
                            p.add("vector",
                                  (lambda v, qd=qd: v.memset(rs8a[qd][:, :], 0.0)),
                                  writes=[rs8a[qd][:, :]])
                    for i2 in range(2):
                        r0 = g * 512 + eorder[i2] * 128
                        dma("gpsimd_d", x1b[i2 % 2][:, :], sx1_d.ap()[r0:r0 + 128, :])

                def p6_chain_tail(g):
                    for eh in range(2):
                        gb = gmx[0]
                        if eh == 1:
                            gmx_load(1, g)
                        for e4 in range(4):
                            e = eh * 4 + e4
                            for hp in range(4):
                                mb = msk[(e4 * 4 + hp) % 2]
                                gsl = gb[:, hp * 2048 + e4 * 512:hp * 2048 + (e4 + 1) * 512]
                                osl = onorm[:, hp * 512:(hp + 1) * 512]
                                p.add("vector", (lambda v, mb=mb, gsl=gsl, osl=osl:
                                                 v.tensor_tensor(
                                    mb[:, :], osl, gsl, Alu.mult)),
                                      reads=[osl, gsl], writes=[mb[:, :]])
                                first = (eh == 0 and e4 == 0 and hp == 0)
                                last = (eh == 1 and e4 == 3 and hp == 3)
                                def o2(t, mb=mb, e=e, hp=hp, first=first, last=last):
                                    for qb in range(4):
                                        for ch in range(2):
                                            ins = t.matmul(
                                                ps8[qb * 2 + ch][:, :],
                                                mb[:, qb * 128:(qb + 1) * 128],
                                                woes[:, (e * 4 + hp) * 1024 + ch * 512:
                                                     (e * 4 + hp) * 1024 + (ch + 1) * 512],
                                                start=first, stop=last)
                                    return ins
                                p.add("tensor", o2,
                                      reads=[mb[:, :],
                                             woes[:, (e * 4 + hp) * 1024:
                                                  (e * 4 + hp + 1) * 1024]],
                                      writes=[ps8[i][:, :] for i in range(8)])
                    for i2, qb in enumerate(eorder):
                        bo = i2 % 2
                        r0 = g * 512 + qb * 128
                        for ch in range(2):
                            p.add("vector", (lambda v, bo=bo, qb=qb, ch=ch:
                                             v.tensor_tensor(
                                x1b[bo][:, ch * 512:(ch + 1) * 512],
                                ps8[qb * 2 + ch][:, :],
                                x1b[bo][:, ch * 512:(ch + 1) * 512], Alu.add)),
                                  reads=[ps8[qb * 2 + ch][:, :],
                                         x1b[bo][:, ch * 512:(ch + 1) * 512]],
                                  writes=[x1b[bo][:, ch * 512:(ch + 1) * 512]])
                        dma("scalar_d", y_d.ap()[r0:r0 + 128, :], x1b[bo][:, :])
                        if i2 + 2 < 4:
                            rn = g * 512 + eorder[i2 + 2] * 128
                            dma("gpsimd_d", x1b[(i2 + 2) % 2][:, :],
                                sx1_d.ap()[rn:rn + 128, :])

                for qd in range(2):
                    p.add("vector", (lambda v, qd=qd: v.memset(rs8a[qd][:, :], 0.0)),
                          writes=[rs8a[qd][:, :]])
                for qg in range(8):
                    nkb = 4 * (qg + 1)
                    bq = qg % 2
                    if qg == 0:
                        gmx_load(0, 0)
                    pool = [ps8[0], ps8[1], ps8[6], ps8[7]]
                    pctr = [0]

                    def p6_sims(kb, hs=range(NH)):
                        for h in hs:
                            hp, hh = h // 2, h % 2
                            sl = at[(kb * 8 + h) % 16]
                            pss = pool[pctr[0] % 4][:, :]
                            pctr[0] += 1
                            lhs = kfm[hh * 64:(hh + 1) * 64,
                                      hp * T + kb * 128:hp * T + (kb + 1) * 128]
                            rhs = qg_t[bq][hh * 64:(hh + 1) * 64,
                                           hp * 512:(hp + 1) * 512]
                            p.add("tensor",
                                  (lambda t, ps2=pss, l=lhs, r=rhs, hh=hh: t.matmul(
                                      ps2, l, r, start=True, stop=True,
                                      tile_position=(hh * 64, 0))),
                                  reads=[lhs, rhs], writes=[pss])
                            p.add("scalar",
                                  (lambda s, d=sl, ps2=pss, h=h: s.activation(
                                      d[:, :], ps2, Act.Exp,
                                      bias=mbias[:, h:h + 1])),
                                  reads=[pss, mbias[:, h:h + 1]], writes=[sl[:, :]])
                            if kb >= 4 * qg:
                                j = kb - 4 * qg
                                p.add("vector",
                                      (lambda v, d=sl, j=j: v.tensor_tensor(
                                          d[:, :], d[:, :],
                                          tri[:, j * 512:(j + 1) * 512], Alu.mult)),
                                      reads=[sl[:, :], tri[:, j * 512:(j + 1) * 512]],
                                      writes=[sl[:, :]])

                    def p6_consume_av(kb, hps):
                        first = (kb == 0)
                        last = (kb == nkb - 1)
                        for hp in hps:
                            pss = ps8[2 + hp]
                            a0 = at[(kb * 8 + 2 * hp) % 16]
                            a1 = at[(kb * 8 + 2 * hp + 1) % 16]
                            v0 = vals[:, (kb * NH + 2 * hp) * 64:(kb * NH + 2 * hp + 1) * 64]
                            v1 = vals[:, (kb * NH + 2 * hp + 1) * 64:(kb * NH + 2 * hp + 2) * 64]
                            def av(t, pss=pss, a0=a0, a1=a1, v0=v0, v1=v1,
                                   first=first, last=last):
                                t.matmul(pss[0:64, :], v0, a0[:, :],
                                         start=first, stop=last,
                                         tile_position=(0, 0),
                                         skip_group_check=True)
                                return t.matmul(pss[64:128, :], v1, a1[:, :],
                                               start=first, stop=last,
                                               tile_position=(0, 64),
                                               skip_group_check=True)
                            p.add("tensor", av,
                                  reads=[a0[:, :], a1[:, :], v0, v1],
                                  writes=[pss[:, :]])

                    def p6_consume_sums(kb):
                        for qd in range(2):
                            pss = pool[pctr[0] % 4]
                            pctr[0] += 1
                            ats = [at[(kb * 8 + qd * 4 + j) % 16] for j in range(4)]
                            def sm(t, pss=pss, ats=ats):
                                for j in range(4):
                                    ins = t.matmul(pss[32 * j:32 * j + 32, :],
                                                   ones32b[:, :], ats[j][:, :],
                                                   start=True, stop=True,
                                                   tile_position=(0, 32 * j),
                                                   skip_group_check=True)
                                return ins
                            p.add("tensor", sm,
                                  reads=[a[:, :] for a in ats] + [ones32b[:, :]],
                                  writes=[pss[:, :]])
                            racc = rs8a[qd][:, :]
                            p.add("vector",
                                  (lambda v, r=racc, ps2=pss: v.tensor_tensor(
                                      r, ps2[:, :], r, Alu.add)),
                                  reads=[pss[:, :], racc], writes=[racc])

                    if qg >= 1:
                        p6_chain_head(qg - 1)
                    def p6_kb(kb):
                        if kb == 1 and qg >= 1:
                            p6_sims(kb)
                            p6_chain_tail(qg - 1)
                            gmx_load(0, qg)
                            p6_consume_av(kb - 1, (0, 1, 2, 3))
                            p6_consume_sums(kb - 1)
                        elif kb >= 1:
                            p6_sims(kb, range(0, 2))
                            p6_consume_av(kb - 1, (0,))
                            p6_sims(kb, range(2, 4))
                            p6_consume_av(kb - 1, (1,))
                            p6_sims(kb, range(4, 6))
                            p6_consume_av(kb - 1, (2,))
                            p6_sims(kb, range(6, 8))
                            p6_consume_av(kb - 1, (3,))
                            p6_consume_sums(kb - 1)
                        else:
                            p6_sims(kb)

                    for kb in range(nkb):
                        p6_kb(kb)
                        if kb == 2 and qg + 1 < 8:
                            for hp in range(4):
                                dma("sync", qg_t[(qg + 1) % 2]
                                    [:, hp * 512:(hp + 1) * 512],
                                    sq_d.ap()[hp, :, (qg + 1) * 512:(qg + 2) * 512])
                    p6_consume_av(nkb - 1, (0, 1, 2, 3))
                    p6_consume_sums(nkb - 1)
                p6_chain_head(7)
                p6_chain_tail(7)

        p.emit()
    return nc


def _host_inputs(inputs):
    g = lambda k: np.asarray(inputs[k], dtype=np.float32)
    x = np.ascontiguousarray(g("L0_attn_x"))
    com = {
        "winT": np.ascontiguousarray(g("L0_attn_w_in").T),
        "wgT": np.ascontiguousarray(g("L0_attn_w_gates").T),
        "woT": np.ascontiguousarray(g("L0_attn_w_out").T),
        "wcv": np.ascontiguousarray(
            g("L0_attn_w_conv").reshape(NCC, 128, 4).transpose(1, 0, 2)),
        "bcv": np.ascontiguousarray(g("L0_attn_b_conv").reshape(NCC, 128).T),
        "bga": np.ascontiguousarray(g("L0_attn_b_gates").reshape(16, 128).T),
        "fb": np.ascontiguousarray(g("L0_attn_forget_base").reshape(NCC, 128).T),
        "wq": np.ascontiguousarray(g("L0_ffn_w_q")),
        "wk": np.ascontiguousarray(g("L0_ffn_w_k")),
        "wgate": np.ascontiguousarray(g("L0_ffn_w_gate")),
        "wv": np.ascontiguousarray(
            np.transpose(g("L0_ffn_value_experts"), (2, 1, 0, 3)).reshape(D, NH * NE * DH)),
        "woe": np.ascontiguousarray(
            g("L0_ffn_output_experts").reshape(NE, 4, 2, DH, D).reshape(NE, 4, 128, D)),
        "c_ident": np.eye(128, dtype=np.float32),
        "c_ones": np.ones((128, 32), dtype=np.float32),
        "c_tri": np.stack([
            (np.arange(512)[None, :] >= (j * 128 + np.arange(128))[:, None])
            .astype(np.float32) for j in range(4)]),
    }
    return x, com


def kernel(**inputs):
    x, com = _host_inputs(inputs)
    nc = _build()
    in_maps = [dict(com, x=np.ascontiguousarray(x[b])) for b in range(8)]
    res = run_bass_kernel_spmd(nc, in_maps, core_ids=list(range(8)))
    return np.stack([r["y"] for r in res.results], axis=0)
